# version 11
# baseline (speedup 1.0000x reference)
import math
from contextlib import ExitStack

import numpy as np
import concourse.bass as bass
import concourse.mybir as mybir
from concourse.bass_utils import run_bass_kernel_spmd

F32 = mybir.dt.float32
BF16 = mybir.dt.bfloat16
I32 = mybir.dt.int32
AF = mybir.ActivationFunctionType
ALU = mybir.AluOpType
AX = mybir.AxisListType.X

P = 128
T = 2048
D = 4096
KC = 32
DKV = 1024
NKV = 8
NH = 32
HID = 11008
HC = 86
NTT = 16
NTB = 4
EPS = 1e-5
EXP_SCALE = 1.0 / math.sqrt(128.0)
EXP_BIAS = -20.0

ENGINES = ("sync", "tensor", "vector", "scalar", "gpsimd")
DBG_SKIP = set()

C_EXPSC = 0
C_EXPBI = 1
C_INVD = 2
C_EPS = 3
C_NHALF = 4
C_NEG1 = 5


class Sem:

    def __init__(self, h):
        self.h = h
        self.c = 0

    def bump(self, n=1):
        self.c += n
        return self.c


class Builder:
    def __init__(self, debug=()):
        self.debug = set(debug)
        nc = bass.Bass("TRN2", target_bir_lowering=False)
        self.nc = nc
        self.ops = {e: [] for e in ENGINES}
        self.es = ExitStack()

        d = {}
        d["x"] = nc.dram_tensor("x", [T, D], F32, kind="ExternalInput").ap()
        d["wq"] = nc.dram_tensor("wq", [D, D], F32, kind="ExternalInput").ap()
        d["wk"] = nc.dram_tensor("wk", [D, DKV], F32, kind="ExternalInput").ap()
        d["wv"] = nc.dram_tensor("wv", [D, DKV], F32, kind="ExternalInput").ap()
        d["wo"] = nc.dram_tensor("wo", [D, D], F32, kind="ExternalInput").ap()
        d["qwg"] = nc.dram_tensor("qwg", [D, HID], I32, kind="ExternalInput").ap()
        d["qwu"] = nc.dram_tensor("qwu", [D, HID], I32, kind="ExternalInput").ap()
        d["qwd"] = nc.dram_tensor("qwd", [HID, D], I32, kind="ExternalInput").ap()
        d["zg"] = nc.dram_tensor("zg", [32, HID], I32, kind="ExternalInput").ap()
        d["sg"] = nc.dram_tensor("sg", [32, HID], F32, kind="ExternalInput").ap()
        d["zu"] = nc.dram_tensor("zu", [32, HID], I32, kind="ExternalInput").ap()
        d["su"] = nc.dram_tensor("su", [32, HID], F32, kind="ExternalInput").ap()
        d["zd"] = nc.dram_tensor("zd", [HC, D], I32, kind="ExternalInput").ap()
        d["sd"] = nc.dram_tensor("sd", [HC, D], F32, kind="ExternalInput").ap()
        d["cosq"] = nc.dram_tensor("cosq", [64, T], F32, kind="ExternalInput").ap()
        d["sinq"] = nc.dram_tensor("sinq", [64, T], F32, kind="ExternalInput").ap()
        d["out"] = nc.dram_tensor("out", [T, D], F32, kind="ExternalOutput").ap()

        def scratch(name, shape, dtype):
            if name in self.debug and dtype is BF16:
                t = nc.dram_tensor("s_" + name, shape, mybir.dt.uint16,
                                   kind="ExternalOutput").ap()
                return t.bitcast(BF16)
            kind = "ExternalOutput" if name in self.debug else "Internal"
            return nc.dram_tensor("s_" + name, shape, dtype, kind=kind).ap()

        d["s_xT"] = scratch("xT", [KC, P, T], BF16)
        d["s_k"] = scratch("k", [NKV, P, T], BF16)
        d["s_v"] = scratch("v", [NTT, P, DKV], BF16)
        d["s_q"] = scratch("q", [NH, P, T], BF16)
        d["s_ctx"] = scratch("ctx", [NH, P, T], BF16)
        d["s_xres"] = scratch("xres", [KC, P, T], BF16)
        d["s_ab"] = scratch("ab", [2, T], BF16)
        d["s_h"] = scratch("h", [HC, P, T], BF16)
        d["s_wd"] = scratch("wd", [HC, P, D], BF16)
        d["s_zsg"] = scratch("zsg", [HC, 2 * D], BF16)
        d["s_zsu"] = scratch("zsu", [HC, 2 * D], BF16)
        d["s_zsd"] = scratch("zsd", [HC, 2 * D], BF16)
        d["s_ln2"] = scratch("ln2", [2, T], F32)
        d["s_recip"] = scratch("recip", [NH, T], BF16)
        if "att" in self.debug:
            d["s_att"] = nc.dram_tensor("s_att", [16, P, 512], F32,
                                        kind="ExternalOutput").ap()
        if "x1" in self.debug:
            d["s_x1"] = scratch("x1", [KC, P, T], BF16)
        if "x2" in self.debug:
            d["s_x2"] = scratch("x2", [KC, P, T], BF16)
        self.d = d

        sb = self.es.enter_context
        self.A = sb(nc.sbuf_tensor("arenaA", [P, 65536], BF16)).ap()
        self.B = sb(nc.sbuf_tensor("arenaB", [P, 32768], BF16)).ap()
        self.consts = sb(nc.sbuf_tensor("consts", [P, 16], F32)).ap()
        self.ones = sb(nc.sbuf_tensor("ones", [P, 8], BF16)).ap()
        self.onesrow = sb(nc.sbuf_tensor("onesrow", [1, 128], BF16)).ap()
        self.ident = sb(nc.sbuf_tensor("ident", [P, P], BF16)).ap()
        self.masks = sb(nc.sbuf_tensor("masks", [P, 4 * 512], BF16)).ap()
        self.cos = sb(nc.sbuf_tensor("cos", [P, 2048], BF16)).ap()
        self.sin = sb(nc.sbuf_tensor("sin", [P, 2048], BF16)).ap()
        self.psA = sb(nc.psum_tensor("psA", [P, 2048], F32)).ap()
        self.psB = sb(nc.psum_tensor("psB", [P, 2048], F32)).ap()
        self.psA_bf = self.psA.bitcast(BF16)
        self.psB_bf = self.psB.bitcast(BF16)

        self.sem = {}
        for name in ("ld", "ldb", "ldc", "ld3", "ld4", "ld5", "ld6", "ld7", "pe", "act", "dve", "pool", "st", "stb", "bar"):
            self.sem[name] = Sem(self.es.enter_context(nc.semaphore(name)))

        self._p1_specs = ([("v", d["wv"], mc) for mc in range(NKV)]
                          + [("k", d["wk"], mc) for mc in range(NKV)]
                          + [("q", d["wq"], mc) for mc in range(NH)])
        self._p1_pf = [self.B[:, 0:4096].bitcast(F32),
                       self.B[:, 4096:8192].bitcast(F32),
                       self.B[:, 8192:12288].bitcast(F32)]
        self._p1_cast_tok = {}
        self._p1_ld_tok = {}
        self._p1_next_half = 0
        self._t_cossin = 0

    def _op(self, engn, meth, *a, waits=(), inc=None, incn=1, **kw):
        waits = tuple(waits)

        def f(e, meth=meth, a=a, kw=kw, waits=waits, inc=inc, incn=incn):
            for (h, v) in waits:
                e.wait_ge(h, v)
            r = getattr(e, meth)(*a, **kw)
            if inc is not None:
                r.then_inc(inc, incn)

        self.ops[engn].append(f)

    def _wait(self, engn, sem, val):
        def f(e, h=sem.h, v=val):
            e.wait_ge(h, v)

        self.ops[engn].append(f)

    def w(self, *pairs):
        return [(s.h, v) for (s, v) in pairs]

    def dma(self, engn, out, in_, waits=(), sem=None, **kw):
        s = sem or self.sem["ld"]
        tok = s.bump(16)
        self._op(engn, "dma_start", waits=waits, inc=s.h, incn=16, out=out, in_=in_, **kw)
        return tok

    def mm(self, out, lhsT, rhs, start, stop, waits=(), inc=False):
        s = self.sem["pe"]
        tok = s.bump() if inc else None
        self._op("tensor", "matmul", out, lhsT, rhs, waits=waits,
                 inc=(s.h if inc else None), start=start, stop=stop)
        return tok

    def tr(self, out, in_, waits=(), inc=False):
        s = self.sem["pe"]
        tok = s.bump() if inc else None
        self._op("tensor", "transpose", out, in_, self.ident, waits=waits,
                 inc=(s.h if inc else None))
        return tok

    def _chained(self, engn, semname, meth, *a, waits=(), **kw):
        s = self.sem[semname]
        wl = list(waits) + [(s.h, s.c)]
        tok = s.bump()
        self._op(engn, meth, *a, waits=wl, inc=s.h, **kw)
        return tok

    def dve(self, meth, *a, waits=(), **kw):
        return self._chained("vector", "dve", meth, *a, waits=waits, **kw)

    def act(self, meth, *a, waits=(), **kw):
        return self._chained("scalar", "act", meth, *a, waits=waits, **kw)

    def pool(self, meth, *a, waits=(), **kw):
        return self._chained("gpsimd", "pool", meth, *a, waits=waits, **kw)

    def load_act_set(self, set_id):
        def f(e, set_id=set_id):
            e.add_instruction(mybir.InstLoadActFuncSet(
                name=self.nc.get_next_instruction_name(),
                act_func_set_id=set_id, ins=[], outs=[]))
        self.ops["scalar"].append(f)

    def barrier(self):
        for nm in ("ld", "ldb", "ldc", "ld3", "ld4", "ld5", "ld6", "ld7", "st", "stb"):
            sm = self.sem[nm]
            if sm.c:
                self._wait("sync", sm, sm.c)
        s = self.sem["bar"]
        base = s.c
        for engn in ENGINES:
            self._op(engn, "nop", inc=s.h)
        s.c = base + 5
        tgt = s.c
        for engn in ENGINES:
            self._wait(engn, s, tgt)

    def a(self, lo, n, dt=BF16):
        v = self.A[:, lo:lo + n]
        return v if dt is BF16 else v.bitcast(dt)

    def b(self, lo, n, dt=BF16):
        v = self.B[:, lo:lo + n]
        return v if dt is BF16 else v.bitcast(dt)

    def p0(self):
        d, A, B = self.d, self.A, self.B
        S = self.sem
        for col, val in ((C_EXPSC, EXP_SCALE), (C_EXPBI, EXP_BIAS), (C_INVD, 1.0 / 4096.0),
                         (C_EPS, EPS), (C_NHALF, -0.5), (C_NEG1, -1.0)):
            self.pool("memset", self.consts[:, col:col + 1], val)
        self.pool("memset", self.ones[:, :], 1.0)
        self.pool("memset", self.onesrow[:, :], 1.0)
        self.pool("memset", self.ident[:, :], 0.0)
        self.pool("affine_select", out=self.ident[:, :], in_=self.ident[:, :],
                  compare_op=ALU.not_equal, fill=1.0, base=0,
                  pattern=[[-1, P]], channel_multiplier=1)
        self.pool("memset", self.masks[:, :], 1.0)
        for j in range(4):
            t_pool_const = self.pool(
                "affine_select", out=self.masks[:, j * 512:(j + 1) * 512],
                in_=self.masks[:, j * 512:(j + 1) * 512],
                compare_op=ALU.is_ge, fill=0.0, base=-128 * j,
                pattern=[[1, 512]], channel_multiplier=-1)
        self._t_pool_const = t_pool_const
        self.dma("gpsimd", self.cos[0:64, :], d["cosq"], sem=S["ldc"])
        self._t_cossin = self.dma("gpsimd", self.sin[0:64, :], d["sinq"], sem=S["ldc"])
        self.load_act_set(6)

        for src, dst, half in ((d["zg"], d["s_zsg"], 0), (d["sg"], d["s_zsg"], 1),
                               (d["zu"], d["s_zsu"], 0), (d["su"], d["s_zsu"], 1)):
            self.dma("gpsimd",
                     dst.rearrange("hc (half g j) -> g hc half j", half=2, j=128)[:, :, half, :],
                     src.rearrange("g (hc j) -> g hc j", j=128), sem=S["ldc"])
        for src, dst, half in ((d["zd"], d["s_zsd"], 0), (d["sd"], d["s_zsd"], 1)):
            self.dma("gpsimd",
                     dst.rearrange("hc (half j) -> hc half j", half=2)[:, half, :],
                     src, sem=S["ldc"])

        xbf = [B[:, 24576:28672], B[:, 28672:32768]]
        Ar = A.rearrange("p (kc t) -> p kc t", t=T)
        psbf = [self.psA_bf, self.psB_bf]
        xbf_free = [(S["pe"].h, 0), (S["pe"].h, 0)]
        ps_free = [(S["dve"].h, 0), (S["dve"].h, 0)]
        ldp = [S["ld3"], S["ld4"]]
        for tt in range(NTT):
            par = tt % 2
            wl = [xbf_free[par]]
            t_ld = self.dma("gpsimd", xbf[par],
                            d["x"][tt * 128:(tt + 1) * 128, :],
                            waits=wl, sem=ldp[par])
            last_tr = None
            for g8 in range(8):
                pp = g8 % 2
                trw = [(ldp[par].h, t_ld), ps_free[pp]]
                if tt == 0 and g8 == 0:
                    trw.append((S["pool"].h, self._t_pool_const))
                for j in range(4):
                    kc = 4 * g8 + j
                    last_tr = self.tr(psbf[pp][:, j * 128:(j + 1) * 128],
                                      xbf[par][:, kc * 128:(kc + 1) * 128],
                                      waits=(trw if j == 0 else ()), inc=(j == 3))
                t_ev = self.dve(
                    "tensor_copy",
                    Ar[:, 4 * g8:4 * g8 + 4, tt * 128:(tt + 1) * 128],
                    psbf[pp][:, 0:512].rearrange("p (j m) -> p j m", m=128),
                    waits=self.w((S["pe"], last_tr)))
                ps_free[pp] = (S["dve"].h, t_ev)
            xbf_free[par] = (S["pe"].h, last_tr)
        for q in range(3):
            self._p1_issue_half(q)
        self._p1_next_half = 3
        t_all_ev = S["dve"].c
        for kc in range(KC):
            self.dma("sync", d["s_xT"][kc], A[:, kc * T:(kc + 1) * T],
                     waits=(self.w((S["dve"], t_all_ev)) if kc == 0 else ()),
                     sem=S["st"])
        self.barrier()

    P1_SPECS = None

    def _p1_issue_half(self, q):
        S = self.sem
        idx, h = q // 2, q % 2
        kind, w_ap, mc = self._p1_specs[idx]
        wr = w_ap.rearrange("(kc p) m -> p kc m", p=P)
        dst = self._p1_pf[q % 3].rearrange("p (kc m) -> p kc m", m=128)
        wl = (self.w((S["dve"], self._p1_cast_tok[q - 3]))
              if (q - 3) in self._p1_cast_tok else ())
        t0 = self.dma("sync", dst[:, 0:8, :],
                      wr[:, 16 * h:16 * h + 8, mc * 128:(mc + 1) * 128],
                      waits=wl, sem=S["ld"])
        t1 = self.dma("scalar", dst[:, 8:16, :],
                      wr[:, 16 * h + 8:16 * h + 16, mc * 128:(mc + 1) * 128],
                      waits=wl, sem=S["ldb"])
        self._p1_ld_tok[q] = (t0, t1)

    def p1(self):
        d, A, B = self.d, self.A, self.B
        S = self.sem
        pbf = [B[:, 12288:16384], B[:, 16384:20480]]
        vo = B[:, 20480:22528]
        ro = B[:, 22528:24576]
        tmpA = B[0:64, 24576:26624]
        vob = B[0:64, 26624:28672]
        vst = [B[:, 28672:29184], B[:, 29184:29696]]
        psum2 = [self.psA, self.psB]
        specs = self._p1_specs
        NSP = len(specs)

        cast_tok = self._p1_cast_tok
        t_evac = [0, 0]
        mm_last = {}
        tr_done = 0
        vst_tok = [0, 0]
        vst_st = [0, 0]
        t_ro_dma = 0
        t_rope_up = 0
        t_upd_prev = 0
        t_vob = 0
        t_sv = 0
        for idx, (kind, w_ap, mc) in enumerate(specs):
            par = idx % 2
            ps = 0 if kind == "v" else idx % 2
            pan_r = pbf[par].rearrange("p (kc m) -> p kc m", m=128)
            last = None
            for h in range(2):
                q = 2 * idx + h
                t0, t1 = self._p1_ld_tok[q]
                cw = self.w((S["ld"], t0), (S["ldb"], t1))
                if (idx - 2) in mm_last:
                    cw += self.w((S["pe"], mm_last[idx - 2]))
                cast_tok[q] = self.dve("tensor_copy",
                                       pbf[par][:, h * 2048:(h + 1) * 2048],
                                       self._p1_pf[q % 3], waits=cw)
                if self._p1_next_half < 2 * NSP:
                    self._p1_issue_half(self._p1_next_half)
                    self._p1_next_half += 1
                for kc in range(16 * h, 16 * h + 16):
                    for tb in range(NTB):
                        wl = []
                        if kc == 16 * h and tb == 0:
                            wl = self.w((S["dve"], cast_tok[q]))
                            if h == 0:
                                wl += self.w((S["dve"], t_evac[ps]))
                        last = self.mm(psum2[ps][:, tb * 512:(tb + 1) * 512],
                                       pan_r[:, kc, :],
                                       A[:, kc * T + tb * 512: kc * T + (tb + 1) * 512],
                                       start=(kc == 0), stop=(kc == KC - 1),
                                       waits=wl,
                                       inc=(kc == 16 * h + 15 and tb == NTB - 1))
            mm_last[idx] = last
            if kind == "v":
                t_ev = self.dve("tensor_copy", vo, psum2[0][:, :],
                                waits=self.w((S["pe"], last), (S["pe"], tr_done)))
                for g4 in range(4):
                    pp = g4 % 2
                    pslo = 1024 * pp
                    for j in range(4):
                        tt = 4 * g4 + j
                        trw = self.w((S["dve"], vst_tok[pp])) if j == 0 else []
                        if g4 == 0 and j == 0:
                            trw += self.w((S["dve"], t_ev))
                        tr_done = self.tr(
                            self.psB_bf[:, pslo + j * 128:pslo + (j + 1) * 128],
                            vo[:, tt * 128:(tt + 1) * 128],
                            waits=trw, inc=(j == 3))
                    t_sv = self.dve("tensor_copy", vst[pp],
                                    self.psB_bf[:, pslo:pslo + 512],
                                    waits=self.w((S["pe"], tr_done),
                                                 (S["st"], vst_st[pp])))
                    vst_tok[pp] = t_sv
                    vst_st[pp] = self.dma(
                        "gpsimd",
                        d["s_v"][4 * g4:4 * g4 + 4, :, mc * 128:(mc + 1) * 128]
                        .rearrange("tt p m -> p tt m"),
                        vst[pp].rearrange("p (tt m) -> p tt m", m=128),
                        waits=self.w((S["dve"], t_sv)), sem=S["st"])
                t_evac[0] = t_ev
                t_evac[1] = t_sv
            else:
                t_ev = self.dve("tensor_copy", vo, psum2[ps][:, :],
                                waits=self.w((S["pe"], last), (S["ld5"], t_vob)))
                k0 = vo[0:64, :]
                c, s = self.cos[0:64, :], self.sin[0:64, :]
                t_vob = self.dma("gpsimd", vob, vo[64:128, :],
                                 waits=self.w((S["dve"], t_ev),
                                              (S["dve"], t_rope_up)), sem=S["ld5"])
                self.dve("tensor_tensor", tmpA, k0, c, ALU.mult,
                         waits=self.w((S["ld5"], t_upd_prev),
                                      (S["ldc"], self._t_cossin)))
                self.dve("tensor_tensor", ro[0:64, :], vob, s, ALU.mult,
                         waits=self.w((S["ld5"], t_vob), (S["stb"], t_ro_dma)))
                self.dve("tensor_tensor", ro[0:64, :], tmpA, ro[0:64, :],
                         ALU.subtract)
                self.dve("tensor_tensor", tmpA, k0, s, ALU.mult)
                self.dve("tensor_tensor", vob, vob, c, ALU.mult)
                t_rope_up = self.dve("tensor_tensor", tmpA, tmpA, vob, ALU.add)
                t_upd_prev = self.dma("gpsimd", ro[64:128, :], tmpA,
                                      waits=self.w((S["dve"], t_rope_up)), sem=S["ld5"])
                dst = d["s_k"][mc] if kind == "k" else d["s_q"][mc]
                t_ro_dma = self.dma("scalar", dst, ro,
                                    waits=self.w((S["ld5"], t_upd_prev)), sem=S["stb"])
                t_evac[ps] = t_ev
        self.barrier()

    def _p45_stream(self):
        d, A = self.d, self.A
        S = self.sem
        qd = [A[:, 24576:32768].bitcast(I32), A[:, 32768:40960].bitcast(I32)]
        qd16 = [A[:, 24576:32768].bitcast(mybir.dt.int16)
                .rearrange("p (n two) -> p n two", two=2)[:, :, 0],
                A[:, 32768:40960].bitcast(mybir.dt.int16)
                .rearrange("p (n two) -> p n two", two=2)[:, :, 0]]
        zsr = [A[:, 40960:49152], A[:, 49152:57344]]
        wdb = [A[:, 57344:61440], A[:, 61440:65536]]
        t_tt1 = [0, 0]
        t_tt2 = [0, 0]
        wdb_st = [0, 0]
        for hc in range(HC):
            par = hc % 2
            t_q = self.dma("sync", qd[par], d["qwd"][hc * 128:(hc + 1) * 128, :],
                           waits=self.w((S["pool"], t_tt1[par])), sem=S["ldb"])
            t_zs = self.dma("sync", zsr[par],
                            d["s_zsd"][hc:hc + 1, :].rearrange("o t -> (o t)").partition_broadcast(P),
                            waits=self.w((S["pool"], t_tt2[par])), sem=S["ldb"])
            t_tt1[par] = self.pool("tensor_tensor", wdb[par], qd16[par],
                                   zsr[par][:, 0:4096], ALU.subtract,
                                   waits=self.w((S["ldb"], max(t_q, t_zs)),
                                                (S["ldb"], wdb_st[par])))
            t_tt2[par] = self.pool("tensor_tensor", wdb[par], wdb[par],
                                   zsr[par][:, 4096:8192], ALU.mult)
            wdb_st[par] = self.dma("sync", d["s_wd"][hc], wdb[par],
                                   waits=self.w((S["pool"], t_tt2[par])), sem=S["ldb"])

    def p2(self):
        d, A, B = self.d, self.A, self.B
        S = self.sem
        if "mdump" in DBG_SKIP:
            self._mdump(2)
        self._p45_stream()
        for kv in range(NKV):
            self.dma("sync", B[:, kv * T:(kv + 1) * T], d["s_k"][kv])
        for tt in range(NTT):
            self.dma("sync", B[:, 16384 + tt * DKV:16384 + (tt + 1) * DKV], d["s_v"][tt])
        t_kv = S["ld"].c
        qt_r = A[:, 0:16384].rearrange("p (mc n) -> p mc n", n=512)
        et = [A[:, 16384:17408], A[:, 17408:18432]]
        ctxo = [A[:, 18432 + i * 512:18432 + (i + 1) * 512] for i in range(4)]
        lsum = A[0:1, 20480:21504].bitcast(F32)
        rsum = [A[0:1, 21504:22016], A[0:1, 22016:22528]]
        ps_sc = [self.psA[:, 0:1024], self.psB[:, 0:1024]]
        ps_ctx = [self.psA[:, 1024:1536], self.psB[:, 1024:1536]]
        ps_sum = [self.psA[0:1, 1536:2048], self.psB[0:1, 1536:2048]]

        sc_free = [0, 0]
        et_free = [0, 0]
        t_ctx_evac = [0, 0]
        t_ln = [0, 0]
        rsum_free = [0, 0]
        ctxo_st = [0] * 4
        stp = [S["st"], S["stb"]]
        t_qt_gate = (S["ld"].h, t_kv)
        pending_tail = [None]

        for qt in range(NTB):
            sq = d["s_q"].rearrange("mc p t -> p mc t")
            qw = [t_qt_gate, (S["pe"].h, S["pe"].c)]
            for g in range(4):
                t_q = self.dma("sync", qt_r[:, g * 8:(g + 1) * 8, :],
                               sq[:, g * 8:(g + 1) * 8, qt * 512:(qt + 1) * 512],
                               waits=qw)
                qw = ()
            nk = 4 * qt + 4
            npair = nk // 2
            for h in range(NH):
                kv = h // 4
                cp = h % 2
                ready = {}
                av_last = None
                sum_last = None

                def emit_sc(jp):
                    toks = []
                    for half in range(2):
                        j = 2 * jp + half
                        wl = []
                        if half == 0:
                            wl = self.w((S["act"], sc_free[jp % 2]))
                            if jp == 0:
                                wl += self.w((S["ld"], t_q))
                        toks.append(self.mm(
                            ps_sc[jp % 2][:, half * 512:(half + 1) * 512],
                            B[:, kv * T + j * 128:kv * T + (j + 1) * 128],
                            qt_r[:, h, :], start=True, stop=True,
                            waits=wl, inc=True))
                    return toks[-1]

                def emit_exp(jp, t_sc):
                    wl = self.w((S["pe"], t_sc), (S["pe"], et_free[jp % 2]))
                    tok = self.act("activation", et[jp % 2], ps_sc[jp % 2], AF.Exp,
                                   bias=self.consts[:, C_EXPBI:C_EXPBI + 1],
                                   scale=self.consts[:, C_EXPSC:C_EXPSC + 1],
                                   waits=wl)
                    sc_free[jp % 2] = tok
                    if 2 * jp >= 4 * qt:
                        mo = (2 * jp - 4 * qt) * 512
                        mt = self.dve("tensor_tensor", et[jp % 2], et[jp % 2],
                                      self.masks[:, mo:mo + 1024],
                                      ALU.mult, waits=self.w((S["act"], tok)))
                        ready[jp] = (S["dve"].h, mt)
                    else:
                        ready[jp] = (S["act"].h, tok)

                def emit_av_sum(jp):
                    nonlocal av_last, sum_last
                    for half in range(2):
                        j = 2 * jp + half
                        wl = [ready[jp]] if half == 0 else []
                        if j == 0:
                            wl += self.w((S["dve"], t_ctx_evac[cp]))
                        av_last = self.mm(
                            ps_ctx[cp],
                            B[:, 16384 + j * DKV + kv * 128:16384 + j * DKV + (kv + 1) * 128],
                            et[jp % 2][:, half * 512:(half + 1) * 512],
                            start=(j == 0), stop=(j == nk - 1),
                            waits=wl, inc=False)
                        wl2 = self.w((S["act"], t_ln[cp])) if j == 0 else ()
                        sum_last = self.mm(ps_sum[cp], self.ones[:, 0:1],
                                           et[jp % 2][:, half * 512:(half + 1) * 512],
                                           start=(j == 0), stop=(j == nk - 1),
                                           waits=wl2, inc=True)
                    et_free[jp % 2] = sum_last

                t_sc = {0: emit_sc(0)}
                emit_exp(0, t_sc[0])
                if pending_tail[0] is not None:
                    pending_tail[0]()
                    pending_tail[0] = None
                for jp in range(1, npair):
                    t_sc[jp] = emit_sc(jp)
                    emit_exp(jp, t_sc[jp])
                    emit_av_sum(jp - 1)
                emit_av_sum(npair - 1)

                t_l = self.act("activation", lsum, ps_sum[cp], AF.Ln,
                               waits=self.w((S["pe"], sum_last)))
                self.act("activation", rsum[cp], lsum, AF.Exp,
                         scale=self.consts[0:1, C_NEG1:C_NEG1 + 1],
                         waits=self.w((stp[cp], rsum_free[cp])))
                t_ln[cp] = t_l
                rsum_free[cp] = self.dma(
                    "scalar", d["s_recip"][h:h + 1, qt * 512:(qt + 1) * 512],
                    rsum[cp], sem=stp[cp])

                def mk_tail(h=h, cp=cp, qt=qt, sum_last=sum_last):
                    def f():
                        hs = h % 4
                        tok = self.dve("tensor_copy", ctxo[hs], ps_ctx[cp],
                                       waits=self.w((S["pe"], sum_last),
                                                    (stp[hs % 2], ctxo_st[hs])))
                        t_ctx_evac[cp] = tok
                        ctxo_st[hs] = self.dma(
                            "scalar", d["s_ctx"][h][:, qt * 512:(qt + 1) * 512],
                            ctxo[hs], waits=self.w((S["dve"], tok)), sem=stp[hs % 2])
                    return f
                pending_tail[0] = mk_tail()
            pending_tail[0]()
            pending_tail[0] = None
            t_qt_gate = (stp[(NH - 1) % 2].h, ctxo_st[(NH - 1) % 4])
        if "p2et" in DBG_SKIP:
            dmp = self.A[:, 20480:21504].bitcast(F32)
            seq = [("et0", et[0]), ("et1", et[1]), ("sc0", ps_sc[0]),
                   ("sc1", ps_sc[1]), ("consts", self.consts[:, 0:16]),
                   ("masks", self.masks[:, 0:512]), ("ones", self.onesrow[:, :])]
            for i, (nm, ap) in enumerate(seq):
                wl = self.w((S["ldc"], S["ldc"].c))
                if i == 0:
                    wl += self.w((stp[0], ctxo_st[0]))
                np_, nf = ap.shape[0], ap.shape[1]
                dv = dmp[0:np_, 0:nf]
                td = self.dve("tensor_copy", dv, ap, waits=wl)
                self.dma("sync", d["s_att"][8 + i][0:np_, 0:nf], dv,
                         waits=self.w((S["dve"], td)), sem=S["ldc"])
            td = self.dve("tensor_copy", dmp[0:1, :], ps_sum,
                          waits=self.w((S["ldc"], S["ldc"].c)))
            self.dma("sync", d["s_att"][5][0:1, :], dmp[0:1, :],
                     waits=self.w((S["dve"], td)), sem=S["ldc"])
            td = self.dve("tensor_copy", dmp[0:1, :], rsum,
                          waits=self.w((S["ldc"], S["ldc"].c)))
            self.dma("sync", d["s_att"][6][0:1, :], dmp[0:1, :],
                     waits=self.w((S["dve"], td)), sem=S["ldc"])
        self.barrier()

    def p3(self):
        d, A, B = self.d, self.A, self.B
        S = self.sem
        pf = [B[:, 0:4096].bitcast(F32), B[:, 4096:8192].bitcast(F32)]
        pbf = [B[:, 8192:12288], B[:, 12288:16384]]
        xoc = B[:, 16384:18432]
        xres_pair = [B[:, 18432:20480], B[:, 20480:22528]]
        sqb = [B[:, 22528:23040], B[:, 23040:23552]]
        statsf = B[:, 23552:27648].bitcast(F32)
        recip_rep = B[:, 27648:29696]
        rowsf = B[0:1, 0:12288].bitcast(F32)
        arow = B[0:1, 12288:14336]
        brow = B[0:1, 14336:16384]
        wor = d["wo"].rearrange("(kc p) m -> p kc m", p=P)

        ld_tok = {}
        cast_tok = {}

        def issue_half(q):
            idx, h = q // 2, q % 2
            dst = pf[q % 2].rearrange("p (kc m) -> p kc m", m=128)
            wl = self.w((S["dve"], cast_tok[q - 2])) if (q - 2) in cast_tok else ()
            t0 = self.dma("sync", dst[:, 0:8, :],
                          wor[:, 16 * h:16 * h + 8, idx * 128:(idx + 1) * 128],
                          waits=wl, sem=S["ld"])
            t1 = self.dma("scalar", dst[:, 8:16, :],
                          wor[:, 16 * h + 8:16 * h + 16, idx * 128:(idx + 1) * 128],
                          waits=wl, sem=S["ldb"])
            ld_tok[q] = (t0, t1)

        issue_half(0)
        issue_half(1)
        next_half = [2]
        t_rn = {}
        t_prev = 0
        for kc in range(KC):
            self.dma("gpsimd", A[:, kc * T:(kc + 1) * T], d["s_ctx"][kc], sem=S["ldc"])
            t_r = self.dma("gpsimd", recip_rep,
                           d["s_recip"][kc:kc + 1, :].rearrange("o t -> (o t)")
                           .partition_broadcast(P),
                           waits=(self.w((S["pool"], t_prev)) if kc else ()),
                           sem=S["ldc"])
            t_prev = self.pool("tensor_tensor", A[:, kc * T:(kc + 1) * T],
                               A[:, kc * T:(kc + 1) * T], recip_rep, ALU.mult,
                               waits=self.w((S["ldc"], t_r)))
            t_rn[kc] = t_prev

        st_ = {"evac": [0] * NTB, "summ": [0] * NTB, "sq_last": 0, "xres_st": [0, 0]}
        pending = [None]
        mm_last = {}
        t_xoc = 0
        for oc in range(KC):
            par = oc % 2
            pan_r = pbf[par].rearrange("p (kc m) -> p kc m", m=128)
            for h in range(2):
                q = 2 * oc + h
                t0, t1 = ld_tok[q]
                cw = self.w((S["ld"], t0), (S["ldb"], t1))
                if (oc - 2) in mm_last:
                    cw += self.w((S["pe"], mm_last[oc - 2]))
                cast_tok[q] = self.dve("tensor_copy",
                                       pbf[par][:, h * 2048:(h + 1) * 2048],
                                       pf[q % 2], waits=cw)
                if next_half[0] < 2 * KC:
                    issue_half(next_half[0])
                    next_half[0] += 1
            t_xoc = self.dma("gpsimd", xoc, d["s_xT"][oc],
                             waits=(self.w((S["dve"], st_["evac"][NTB - 1]))
                                    if oc > 0 else ()), sem=S["ldc"])
            mm_toks = [None] * NTB
            te_list = [None] * NTB
            for hp in range(2):
                last = None
                for kc in range(KC):
                    for tb in (2 * hp, 2 * hp + 1):
                        wl = []
                        if kc == 0 and tb == 2 * hp:
                            wl = self.w((S["dve"], cast_tok[2 * oc]),
                                        (S["dve"], st_["evac"][2 * hp + 1]))
                        elif kc == 16 and tb == 2 * hp:
                            wl = self.w((S["dve"], cast_tok[2 * oc + 1]))
                        if oc == 0:
                            wl += self.w((S["pool"], t_rn[kc]))
                        last = self.mm(self.psA[:, tb * 512:(tb + 1) * 512],
                                       pan_r[:, kc, :],
                                       A[:, kc * T + tb * 512:kc * T + (tb + 1) * 512],
                                       start=(kc == 0), stop=(kc == KC - 1),
                                       waits=wl, inc=(kc == KC - 1))
                        mm_toks[tb] = last
                for tb in (2 * hp, 2 * hp + 1):
                    te = self.dve("tensor_tensor",
                                  xres_pair[par][:, tb * 512:(tb + 1) * 512],
                                  self.psA[:, tb * 512:(tb + 1) * 512],
                                  xoc[:, tb * 512:(tb + 1) * 512], ALU.add,
                                  waits=self.w((S["pe"], mm_toks[tb]),
                                               (S["ldc"], t_xoc),
                                               (S["pe"], st_["summ"][tb]),
                                               (S["st"], st_["xres_st"][par])))
                    st_["evac"][tb] = te
                    te_list[tb] = te
                if hp == 0 and pending[0] is not None:
                    pending[0]()
                    pending[0] = None
            mm_last[oc] = mm_toks[NTB - 1]

            def mk(oc=oc, te_list=te_list, xres=xres_pair[par]):
                def f():
                    for tb in range(NTB):
                        tq = self.dve("tensor_tensor", sqb[tb % 2],
                                      xres[:, tb * 512:(tb + 1) * 512],
                                      xres[:, tb * 512:(tb + 1) * 512], ALU.mult,
                                      waits=self.w((S["dve"], te_list[tb]),
                                                   (S["pe"], st_["sq_last"])))
                        st_["summ"][tb] = self.mm(
                            self.psB[0:1, tb * 512:(tb + 1) * 512],
                            self.ones[:, 0:1], xres[:, tb * 512:(tb + 1) * 512],
                            start=(oc == 0), stop=(oc == KC - 1),
                            waits=self.w((S["dve"], te_list[tb])), inc=True)
                        st_["sq_last"] = self.mm(
                            self.psB[32:33, tb * 512:(tb + 1) * 512],
                            self.ones[:, 0:1], sqb[tb % 2],
                            start=(oc == 0), stop=(oc == KC - 1),
                            waits=self.w((S["dve"], tq)), inc=True)
                return f
            pending[0] = mk()
            if oc == KC - 1:
                pending[0]()
                pending[0] = None
            st_["xres_st"][par] = self.dma(
                "gpsimd", d["s_xres"][oc], xres_pair[par],
                waits=self.w((S["dve"], st_["evac"][NTB - 1])), sem=S["st"])
        t_st = self.dve("tensor_copy", statsf[0:64, :], self.psB[0:64, :],
                        waits=self.w((S["pe"], st_["sq_last"])))
        sums, sumsq = statsf[0:1, :], statsf[32:33, :]
        m, ex2 = rowsf[:, 0:2048], rowsf[:, 2048:4096]
        tmp = rowsf[:, 4096:6144]
        c = self.consts
        self.dve("tensor_scalar", m, sums, c[0:1, C_INVD:C_INVD + 1], None, ALU.mult)
        self.dve("tensor_scalar", ex2, sumsq, c[0:1, C_INVD:C_INVD + 1], None, ALU.mult)
        self.dve("tensor_tensor", tmp, m, m, ALU.mult)
        t_var = self.dve("tensor_tensor", ex2, ex2, tmp, ALU.subtract)
        t_l = self.act("activation", tmp, ex2, AF.Ln,
                       bias=c[0:1, C_EPS:C_EPS + 1], waits=self.w((S["dve"], t_var)))
        t_r = self.act("activation", ex2, tmp, AF.Exp,
                       scale=c[0:1, C_NHALF:C_NHALF + 1])
        t_a = self.dve("tensor_copy", arow, ex2, waits=self.w((S["act"], t_r)))
        self.dve("tensor_tensor", tmp, m, ex2, ALU.mult)
        t_b = self.dve("tensor_scalar", brow, tmp, c[0:1, C_NEG1:C_NEG1 + 1], None,
                       ALU.mult)
        self.dma("gpsimd", d["s_ab"][0:1, :], arow, waits=self.w((S["dve"], t_a)),
                 sem=S["st"])
        self.dma("gpsimd", d["s_ab"][1:2, :], brow, waits=self.w((S["dve"], t_b)),
                 sem=S["st"])
        self.barrier()

    def p4pre(self):
        d, A, B = self.d, self.A, self.B
        S = self.sem
        a_rep, b_rep = B[:, 0:2048], B[:, 2048:4096]
        t_a = self.dma("sync", a_rep, d["s_ab"][0:1, :].rearrange("o t -> (o t)").partition_broadcast(P))
        t_b = self.dma("sync", b_rep, d["s_ab"][1:2, :].rearrange("o t -> (o t)").partition_broadcast(P))
        t_lds = [self.dma("sync", A[:, kc * T:(kc + 1) * T], d["s_xres"][kc])
                 for kc in range(KC)]
        for kc in range(KC):
            sl = A[:, kc * T:(kc + 1) * T]
            self.dve("tensor_tensor", sl, sl, a_rep, ALU.mult,
                     waits=self.w((S["ld"], t_lds[-1])))
            tx = self.dve("tensor_tensor", sl, sl, b_rep, ALU.add)
            if "x1" in self.debug:
                self.dma("sync", d["s_x1"][kc], sl, waits=self.w((S["dve"], tx)),
                         sem=S["st"])
        self.barrier()

    def p4(self):
        d, A, B = self.d, self.A, self.B
        S = self.sem
        self.load_act_set(18)
        qbuf = B[:, 0:8192].bitcast(I32)
        qbuf_r = qbuf.rearrange("p (kc m) -> p kc m", m=128)
        qbuf_i16 = B[:, 0:8192].bitcast(mybir.dt.int16) \
            .rearrange("p (n two) -> p n two", two=2)[:, :, 0]
        zpart = B[:, 8192:12288]
        spart = B[:, 12288:16384]
        zs = B[:, 8192:16384]
        wdeq = {"g": B[:, 16384:20480], "u": B[:, 20480:24576]}
        ho = [B[:, 24576:26624], B[:, 26624:28672]]
        sg = B[:, 28672:30720]
        qws = {"g": d["qwg"].rearrange("(kc p) n -> p kc n", p=P),
               "u": d["qwu"].rearrange("(kc p) n -> p kc n", p=P)}
        zss = {"g": d["s_zsg"], "u": d["s_zsu"]}
        psm = {"g": self.psA, "u": self.psB}

        t_tt1 = 0
        t_tt2 = 0
        wdeq_free = {"g": 0, "u": 0}
        silu_done = [0] * NTB
        tth_done = [0] * NTB
        ho_st = [0, 0]
        stp = [S["st"], S["stb"]]
        for hc in range(HC):
            mm_tok = {}
            for m in ("g", "u"):
                ldt = []
                for hh in range(2):
                    ldt.append(self.dma(
                        "sync", qbuf_r[:, 16 * hh:16 * hh + 16, :],
                        qws[m][:, 16 * hh:16 * hh + 16, hc * 128:(hc + 1) * 128],
                        waits=(self.w((S["pool"], t_tt1)) if hh == 0 else ())))
                t_zs = self.dma("sync", zs,
                                zss[m][hc:hc + 1, :].rearrange("o t -> (o t)").partition_broadcast(P),
                                waits=self.w((S["pool"], t_tt2)))
                t_tt1 = self.pool("tensor_tensor", wdeq[m], qbuf_i16, zpart, ALU.subtract,
                                  waits=self.w((S["ld"], max(ldt[-1], t_zs)),
                                               (S["pe"], wdeq_free[m])))
                t_tt2 = self.pool("tensor_tensor", wdeq[m], wdeq[m], spart, ALU.mult)
                toks = []
                for tb in range(NTB):
                    last = None
                    for kc in range(KC):
                        wl = []
                        if kc == 0:
                            wl = self.w((S["pool"], t_tt2))
                            if m == "g":
                                wl += self.w((S["act"], silu_done[tb]))
                            else:
                                wl += self.w((S["dve"], tth_done[tb]))
                        last = self.mm(psm[m][:, tb * 512:(tb + 1) * 512],
                                       wdeq[m][:, kc * 128:(kc + 1) * 128],
                                       A[:, kc * T + tb * 512:kc * T + (tb + 1) * 512],
                                       start=(kc == 0), stop=(kc == KC - 1),
                                       waits=wl, inc=(kc == KC - 1))
                    toks.append(last)
                mm_tok[m] = toks
                wdeq_free[m] = toks[-1]
            par = hc % 2
            tth = []
            for tb in range(NTB):
                ts = self.act("activation", sg[:, tb * 512:(tb + 1) * 512],
                              self.psA[:, tb * 512:(tb + 1) * 512], AF.Silu,
                              waits=self.w((S["pe"], mm_tok["g"][tb]),
                                           (S["dve"], tth_done[tb])))
                silu_done[tb] = ts
                th = self.dve("tensor_tensor", ho[par][:, tb * 512:(tb + 1) * 512],
                              self.psB[:, tb * 512:(tb + 1) * 512],
                              sg[:, tb * 512:(tb + 1) * 512], ALU.mult,
                              waits=self.w((S["pe"], mm_tok["u"][tb]),
                                           (S["act"], ts), (stp[par], ho_st[par])))
                tth_done[tb] = th
                tth.append(th)
            ho_st[par] = self.dma("scalar", d["s_h"][hc], ho[par],
                                  waits=self.w((S["dve"], tth[-1])), sem=stp[par])
        self.barrier()

    def p45(self):
        d, A, B = self.d, self.A, self.B
        S = self.sem
        qd = B[:, 0:8192].bitcast(I32)
        zpart = B[:, 8192:12288]
        spart = B[:, 12288:16384]
        zs = B[:, 8192:16384]
        wdb = [B[:, 16384:20480], B[:, 20480:24576]]
        t_tt1 = 0
        t_tt2 = 0
        wdb_st = [0, 0]
        stp = [S["st"], S["stb"]]
        for hc in range(HC):
            par = hc % 2
            t_q = self.dma("sync", qd, d["qwd"][hc * 128:(hc + 1) * 128, :],
                           waits=self.w((S["dve"], t_tt1)))
            t_zs = self.dma("sync", zs, d["s_zsd"][hc:hc + 1, :].rearrange("o t -> (o t)").partition_broadcast(P),
                            waits=self.w((S["dve"], t_tt2)))
            t_tt1 = self.dve("tensor_tensor", wdb[par], qd, zpart, ALU.subtract,
                             waits=self.w((S["ld"], max(t_q, t_zs)),
                                          (stp[par], wdb_st[par])))
            t_tt2 = self.dve("tensor_tensor", wdb[par], wdb[par], spart, ALU.mult)
            wdb_st[par] = self.dma("sync", d["s_wd"][hc], wdb[par],
                                   waits=self.w((S["dve"], t_tt2)), sem=stp[par])
        self.barrier()

    def p5(self):
        d, A, B = self.d, self.A, self.B
        S = self.sem
        NR = 8
        ht = [B[:, i * 1024:(i + 1) * 1024] for i in range(NR)]
        wt = [B[:, 8192 + i * 512:8192 + (i + 1) * 512] for i in range(NR)]
        psY = [self.psA[:, 0:1024], self.psA[:, 1024:2048],
               self.psB[:, 0:1024], self.psB[:, 1024:2048]]
        buf_free = [0] * NR
        t_ev = [0] * 4
        ldp = [S["ld"], S["ldb"], S["ldc"], S["ld3"], S["ld4"], S["ld5"], S["ld6"], S["ld7"]]
        for dq in range(8):
            for th in range(2):
                hc_tok = {}
                for hc in range(HC):
                    par = hc % NR
                    t_h = self.dma("sync", ht[par],
                                   d["s_h"][hc][:, th * 1024:(th + 1) * 1024],
                                   waits=self.w((S["pe"], buf_free[par])),
                                   sem=ldp[par])
                    t_w = self.dma("scalar", wt[par],
                                   d["s_wd"][hc][:, dq * 512:(dq + 1) * 512],
                                   waits=self.w((S["pe"], buf_free[par])),
                                   sem=ldp[par])
                    last = None
                    for dc in range(4):
                        for tb2 in range(2):
                            wl = []
                            if dc == 0 and tb2 == 0:
                                wl = self.w((ldp[par], max(t_h, t_w)))
                                if hc == 0:
                                    wl += self.w((S["dve"], max(t_ev)))
                            last = self.mm(
                                psY[dc][:, tb2 * 512:(tb2 + 1) * 512],
                                wt[par][:, dc * 128:(dc + 1) * 128],
                                ht[par][:, tb2 * 512:(tb2 + 1) * 512],
                                start=(hc == 0), stop=(hc == HC - 1),
                                waits=wl, inc=(dc == 3 and tb2 == 1))
                    buf_free[par] = last
                    hc_tok[hc] = last
                for dc in range(4):
                    dcg = dq * 4 + dc
                    sl = A[:, dcg * T + th * 1024:dcg * T + (th + 1) * 1024]
                    t_ev[dc] = self.dve("tensor_tensor", sl, psY[dc], sl, ALU.add,
                                        waits=self.w((S["pe"], hc_tok[HC - 1])))
        if "x2" in self.debug:
            tx = S["dve"].c
            for kc in range(KC):
                self.dma("sync", d["s_x2"][kc], A[:, kc * T:(kc + 1) * T],
                         waits=self.w((S["dve"], tx)), sem=S["st"])
        self.barrier()

    def p6(self):
        d, A, B = self.d, self.A, self.B
        S = self.sem
        self.load_act_set(6)
        sq = [B[:, 3072:3584], B[:, 3584:4096]]
        statsf = B[:, 12800:16896].bitcast(F32)
        rowsf = B[0:1, 16896:29184].bitcast(F32)
        c = self.consts
        t_sqmm = [0, 0]
        t_last = None
        for tb in range(NTB):
            for kc in range(KC):
                par = kc % 2
                sl = A[:, kc * T + tb * 512:kc * T + (tb + 1) * 512]
                tq = self.dve("tensor_tensor", sq[par], sl, sl, ALU.mult,
                              waits=self.w((S["pe"], t_sqmm[par])))
                self.mm(self.psA[0:1, tb * 512:(tb + 1) * 512],
                        self.ones[:, 0:1], sl,
                        start=(kc == 0), stop=(kc == KC - 1), waits=(), inc=False)
                t_sqmm[par] = self.mm(self.psA[32:33, tb * 512:(tb + 1) * 512],
                                      self.ones[:, 0:1], sq[par],
                                      start=(kc == 0), stop=(kc == KC - 1),
                                      waits=self.w((S["dve"], tq)), inc=True)
                t_last = t_sqmm[par]
        t_stats = self.dve("tensor_copy", statsf[0:64, :], self.psA[0:64, :],
                           waits=self.w((S["pe"], t_last)))
        sums, sumsq = statsf[0:1, :], statsf[32:33, :]
        m, ex2 = rowsf[:, 0:2048], rowsf[:, 2048:4096]
        tmp = rowsf[:, 4096:6144]
        self.dve("tensor_scalar", m, sums, c[0:1, C_INVD:C_INVD + 1], None, ALU.mult)
        self.dve("tensor_scalar", ex2, sumsq, c[0:1, C_INVD:C_INVD + 1], None, ALU.mult)
        self.dve("tensor_tensor", tmp, m, m, ALU.mult)
        t_var = self.dve("tensor_tensor", ex2, ex2, tmp, ALU.subtract)
        self.act("activation", tmp, ex2, AF.Ln,
                 bias=c[0:1, C_EPS:C_EPS + 1], waits=self.w((S["dve"], t_var)))
        t_r = self.act("activation", ex2, tmp, AF.Exp,
                       scale=c[0:1, C_NHALF:C_NHALF + 1])
        t_nm = self.dve("tensor_tensor", tmp, m, ex2, ALU.mult,
                        waits=self.w((S["act"], t_r)))
        t_nm = self.dve("tensor_scalar", tmp, tmp, c[0:1, C_NEG1:C_NEG1 + 1], None,
                        ALU.mult)
        t_s1 = self.dma("sync", d["s_ln2"][0:1, :], ex2,
                        waits=self.w((S["act"], t_r)), sem=S["st"])
        t_s2 = self.dma("sync", d["s_ln2"][1:2, :], tmp,
                        waits=self.w((S["dve"], t_nm)), sem=S["st"])
        rcol = B[:, 25088:25120].bitcast(F32)
        nmrcol = B[:, 25120:25152].bitcast(F32)
        t_rc = self.dma("sync", rcol,
                        d["s_ln2"][0:1, :].rearrange("o (tt p) -> (o p) tt", p=P),
                        waits=self.w((S["st"], t_s1)),
                        sem=S["ldc"], allow_slow_non_contiguous=True)
        t_nc = self.dma("sync", nmrcol,
                        d["s_ln2"][1:2, :].rearrange("o (tt p) -> (o p) tt", p=P),
                        waits=self.w((S["st"], t_s2)),
                        sem=S["ldc"], allow_slow_non_contiguous=True)
        outb = [B[:, 4096:12288].bitcast(F32), B[:, 16896:25088].bitcast(F32)]
        psbf = [self.psA_bf, self.psB_bf]
        ps_free = [(S["dve"].h, t_stats), (S["dve"].h, t_stats)]
        out_st = [0, 0]
        stp = [S["st"], S["stb"]]
        first = self.w((S["ldc"], max(t_rc, t_nc)))
        for tt in range(NTT):
            par = tt % 2
            t_aff = []
            for kcg in range(8):
                pp = kcg % 2
                last_tr = None
                for j in range(4):
                    kc = 4 * kcg + j
                    last_tr = self.tr(psbf[pp][:, j * 128:(j + 1) * 128],
                                      A[:, kc * T + tt * 128:kc * T + (tt + 1) * 128],
                                      waits=([ps_free[pp]] if j == 0 else ()),
                                      inc=(j == 3))
                wl = first + self.w((S["pe"], last_tr), (stp[par], out_st[par]))
                first = []
                ta = self.dve("tensor_scalar",
                              outb[par][:, kcg * 512:(kcg + 1) * 512],
                              psbf[pp][:, 0:512],
                              rcol[:, tt:tt + 1], nmrcol[:, tt:tt + 1],
                              ALU.mult, ALU.add, waits=wl)
                ps_free[pp] = (S["dve"].h, ta)
                t_aff.append(ta)
            out_st[par] = self.dma("sync", d["out"][tt * 128:(tt + 1) * 128, :],
                                   outb[par], waits=self.w((S["dve"], t_aff[-1])),
                                   sem=stp[par])
        self._wait("sync", S["st"], S["st"].c)
        self._wait("sync", S["stb"], S["stb"].c)
        self.barrier()

    def finalize(self):
        nc = self.nc
        with nc.Block() as block:
            for engn in ENGINES:
                def runner(e, engn=engn):
                    for f in self.ops[engn]:
                        f(e)
                getattr(block, engn)(runner)
        self.es.close()
        return nc


def build(debug=(), nphases=9):
    b = Builder(debug)
    phases = [b.p0, b.p1, b.p2, b.p3, b.p4pre, b.p4, b.p5, b.p6]
    for p in phases[:nphases]:
        p()
    return b.finalize()


def _rope_tables():
    inv = 1.0 / (500000.0 ** (np.arange(0, 128, 2, dtype=np.float64) / 128.0))
    t = np.arange(T, dtype=np.float64)
    freqs = np.outer(t, inv)
    cos = np.cos(freqs).T.astype(np.float32)
    sin = np.sin(freqs).T.astype(np.float32)
    return np.ascontiguousarray(cos), np.ascontiguousarray(sin)


_NC_CACHE = {}


def kernel(L0_attn_x, L0_attn_Wq, L0_attn_Wk, L0_attn_Wv, L0_attn_Wo,
           L0_ffn_qw_gate, L0_ffn_z_gate, L0_ffn_s_gate,
           L0_ffn_qw_up, L0_ffn_z_up, L0_ffn_s_up,
           L0_ffn_qw_down, L0_ffn_z_down, L0_ffn_s_down):
    if "nc" not in _NC_CACHE:
        _NC_CACHE["nc"] = build()
    nc = _NC_CACHE["nc"]
    cos, sin = _rope_tables()
    common = {
        "wq": np.ascontiguousarray(L0_attn_Wq, dtype=np.float32),
        "wk": np.ascontiguousarray(L0_attn_Wk, dtype=np.float32),
        "wv": np.ascontiguousarray(L0_attn_Wv, dtype=np.float32),
        "wo": np.ascontiguousarray(L0_attn_Wo, dtype=np.float32),
        "qwg": np.ascontiguousarray(L0_ffn_qw_gate, dtype=np.int32),
        "zg": np.ascontiguousarray(L0_ffn_z_gate, dtype=np.int32),
        "sg": np.ascontiguousarray(L0_ffn_s_gate, dtype=np.float32),
        "qwu": np.ascontiguousarray(L0_ffn_qw_up, dtype=np.int32),
        "zu": np.ascontiguousarray(L0_ffn_z_up, dtype=np.int32),
        "su": np.ascontiguousarray(L0_ffn_s_up, dtype=np.float32),
        "qwd": np.ascontiguousarray(L0_ffn_qw_down, dtype=np.int32),
        "zd": np.ascontiguousarray(L0_ffn_z_down, dtype=np.int32),
        "sd": np.ascontiguousarray(L0_ffn_s_down, dtype=np.float32),
        "cosq": cos, "sinq": sin,
    }
    x = np.asarray(L0_attn_x, dtype=np.float32)
    in_maps = [dict(common, x=np.ascontiguousarray(x[i])) for i in range(8)]
    res = run_bass_kernel_spmd(nc, in_maps, core_ids=list(range(8)))
    return np.stack([res.results[i]["out"] for i in range(8)], axis=0)


# version 12
# speedup vs baseline: 1.0086x; 1.0086x over previous
import math
from contextlib import ExitStack

import numpy as np
import concourse.bass as bass
import concourse.mybir as mybir
from concourse.bass_utils import run_bass_kernel_spmd

F32 = mybir.dt.float32
BF16 = mybir.dt.bfloat16
I32 = mybir.dt.int32
AF = mybir.ActivationFunctionType
ALU = mybir.AluOpType
AX = mybir.AxisListType.X

P = 128
T = 2048
D = 4096
KC = 32
DKV = 1024
NKV = 8
NH = 32
HID = 11008
HC = 86
NTT = 16
NTB = 4
EPS = 1e-5
EXP_SCALE = 1.0 / math.sqrt(128.0)
EXP_BIAS = -20.0

ENGINES = ("sync", "tensor", "vector", "scalar", "gpsimd")
DBG_SKIP = set()

C_EXPSC = 0
C_EXPBI = 1
C_INVD = 2
C_EPS = 3
C_NHALF = 4
C_NEG1 = 5


class Sem:

    def __init__(self, h):
        self.h = h
        self.c = 0

    def bump(self, n=1):
        self.c += n
        return self.c


class Builder:
    def __init__(self, debug=()):
        self.debug = set(debug)
        nc = bass.Bass("TRN2", target_bir_lowering=False)
        self.nc = nc
        self.ops = {e: [] for e in ENGINES}
        self.es = ExitStack()

        d = {}
        d["x"] = nc.dram_tensor("x", [T, D], F32, kind="ExternalInput").ap()
        d["wq"] = nc.dram_tensor("wq", [D, D], F32, kind="ExternalInput").ap()
        d["wk"] = nc.dram_tensor("wk", [D, DKV], F32, kind="ExternalInput").ap()
        d["wv"] = nc.dram_tensor("wv", [D, DKV], F32, kind="ExternalInput").ap()
        d["wo"] = nc.dram_tensor("wo", [D, D], F32, kind="ExternalInput").ap()
        d["qwg"] = nc.dram_tensor("qwg", [D, HID], I32, kind="ExternalInput").ap()
        d["qwu"] = nc.dram_tensor("qwu", [D, HID], I32, kind="ExternalInput").ap()
        d["qwd"] = nc.dram_tensor("qwd", [HID, D], I32, kind="ExternalInput").ap()
        d["zg"] = nc.dram_tensor("zg", [32, HID], I32, kind="ExternalInput").ap()
        d["sg"] = nc.dram_tensor("sg", [32, HID], F32, kind="ExternalInput").ap()
        d["zu"] = nc.dram_tensor("zu", [32, HID], I32, kind="ExternalInput").ap()
        d["su"] = nc.dram_tensor("su", [32, HID], F32, kind="ExternalInput").ap()
        d["zd"] = nc.dram_tensor("zd", [HC, D], I32, kind="ExternalInput").ap()
        d["sd"] = nc.dram_tensor("sd", [HC, D], F32, kind="ExternalInput").ap()
        d["cosq"] = nc.dram_tensor("cosq", [64, T], F32, kind="ExternalInput").ap()
        d["sinq"] = nc.dram_tensor("sinq", [64, T], F32, kind="ExternalInput").ap()
        d["out"] = nc.dram_tensor("out", [T, D], F32, kind="ExternalOutput").ap()

        def scratch(name, shape, dtype):
            if name in self.debug and dtype is BF16:
                t = nc.dram_tensor("s_" + name, shape, mybir.dt.uint16,
                                   kind="ExternalOutput").ap()
                return t.bitcast(BF16)
            kind = "ExternalOutput" if name in self.debug else "Internal"
            return nc.dram_tensor("s_" + name, shape, dtype, kind=kind).ap()

        d["s_xT"] = scratch("xT", [KC, P, T], BF16)
        d["s_k"] = scratch("k", [NKV, P, T], BF16)
        d["s_v"] = scratch("v", [NTT, P, DKV], BF16)
        d["s_q"] = scratch("q", [NH, P, T], BF16)
        d["s_ctx"] = scratch("ctx", [NH, P, T], BF16)
        d["s_xres"] = scratch("xres", [KC, P, T], BF16)
        d["s_ab"] = scratch("ab", [2, T], BF16)
        d["s_h"] = scratch("h", [HC, P, T], BF16)
        d["s_wd"] = scratch("wd", [HC, P, D], BF16)
        d["s_zsg"] = scratch("zsg", [HC, 2 * D], BF16)
        d["s_zsu"] = scratch("zsu", [HC, 2 * D], BF16)
        d["s_zsd"] = scratch("zsd", [HC, 2 * D], BF16)
        d["s_ln2"] = scratch("ln2", [2, T], F32)
        d["s_recip"] = scratch("recip", [NH, T], BF16)
        if "att" in self.debug:
            d["s_att"] = nc.dram_tensor("s_att", [16, P, 512], F32,
                                        kind="ExternalOutput").ap()
        if "x1" in self.debug:
            d["s_x1"] = scratch("x1", [KC, P, T], BF16)
        if "x2" in self.debug:
            d["s_x2"] = scratch("x2", [KC, P, T], BF16)
        self.d = d

        sb = self.es.enter_context
        self.A = sb(nc.sbuf_tensor("arenaA", [P, 65536], BF16)).ap()
        self.B = sb(nc.sbuf_tensor("arenaB", [P, 32768], BF16)).ap()
        self.consts = sb(nc.sbuf_tensor("consts", [P, 16], F32)).ap()
        self.ones = sb(nc.sbuf_tensor("ones", [P, 8], BF16)).ap()
        self.onesrow = sb(nc.sbuf_tensor("onesrow", [1, 128], BF16)).ap()
        self.ident = sb(nc.sbuf_tensor("ident", [P, P], BF16)).ap()
        self.masks = sb(nc.sbuf_tensor("masks", [P, 4 * 512], BF16)).ap()
        self.cos = sb(nc.sbuf_tensor("cos", [P, 2048], BF16)).ap()
        self.sin = sb(nc.sbuf_tensor("sin", [P, 2048], BF16)).ap()
        self.psA = sb(nc.psum_tensor("psA", [P, 2048], F32)).ap()
        self.psB = sb(nc.psum_tensor("psB", [P, 2048], F32)).ap()
        self.psA_bf = self.psA.bitcast(BF16)
        self.psB_bf = self.psB.bitcast(BF16)

        self.sem = {}
        for name in ("ld", "ldb", "ldc", "ld3", "ld4", "ld5", "ld6", "ld7", "pe", "act", "dve", "pool", "st", "stb", "bar"):
            self.sem[name] = Sem(self.es.enter_context(nc.semaphore(name)))

        self._p1_specs = ([("v", d["wv"], mc) for mc in range(NKV)]
                          + [("k", d["wk"], mc) for mc in range(NKV)]
                          + [("q", d["wq"], mc) for mc in range(NH)])
        self._p1_pf = [self.B[:, 0:4096].bitcast(F32),
                       self.B[:, 4096:8192].bitcast(F32),
                       self.B[:, 8192:12288].bitcast(F32)]
        self._p1_cast_tok = {}
        self._p1_ld_tok = {}
        self._p1_next_half = 0
        self._t_cossin = 0

    def _op(self, engn, meth, *a, waits=(), inc=None, incn=1, **kw):
        waits = tuple(waits)

        def f(e, meth=meth, a=a, kw=kw, waits=waits, inc=inc, incn=incn):
            for (h, v) in waits:
                e.wait_ge(h, v)
            r = getattr(e, meth)(*a, **kw)
            if inc is not None:
                r.then_inc(inc, incn)

        self.ops[engn].append(f)

    def _wait(self, engn, sem, val):
        def f(e, h=sem.h, v=val):
            e.wait_ge(h, v)

        self.ops[engn].append(f)

    def w(self, *pairs):
        return [(s.h, v) for (s, v) in pairs]

    def dma(self, engn, out, in_, waits=(), sem=None, **kw):
        s = sem or self.sem["ld"]
        tok = s.bump(16)
        self._op(engn, "dma_start", waits=waits, inc=s.h, incn=16, out=out, in_=in_, **kw)
        return tok

    def mm(self, out, lhsT, rhs, start, stop, waits=(), inc=False):
        s = self.sem["pe"]
        tok = s.bump() if inc else None
        self._op("tensor", "matmul", out, lhsT, rhs, waits=waits,
                 inc=(s.h if inc else None), start=start, stop=stop)
        return tok

    def tr(self, out, in_, waits=(), inc=False):
        s = self.sem["pe"]
        tok = s.bump() if inc else None
        self._op("tensor", "transpose", out, in_, self.ident, waits=waits,
                 inc=(s.h if inc else None))
        return tok

    def _chained(self, engn, semname, meth, *a, waits=(), **kw):
        s = self.sem[semname]
        wl = list(waits) + [(s.h, s.c)]
        tok = s.bump()
        self._op(engn, meth, *a, waits=wl, inc=s.h, **kw)
        return tok

    def dve(self, meth, *a, waits=(), **kw):
        return self._chained("vector", "dve", meth, *a, waits=waits, **kw)

    def act(self, meth, *a, waits=(), **kw):
        return self._chained("scalar", "act", meth, *a, waits=waits, **kw)

    def pool(self, meth, *a, waits=(), **kw):
        return self._chained("gpsimd", "pool", meth, *a, waits=waits, **kw)

    def load_act_set(self, set_id):
        def f(e, set_id=set_id):
            e.add_instruction(mybir.InstLoadActFuncSet(
                name=self.nc.get_next_instruction_name(),
                act_func_set_id=set_id, ins=[], outs=[]))
        self.ops["scalar"].append(f)

    def barrier(self):
        for nm in ("ld", "ldb", "ldc", "ld3", "ld4", "ld5", "ld6", "ld7", "st", "stb"):
            sm = self.sem[nm]
            if sm.c:
                self._wait("sync", sm, sm.c)
        s = self.sem["bar"]
        base = s.c
        for engn in ENGINES:
            self._op(engn, "nop", inc=s.h)
        s.c = base + 5
        tgt = s.c
        for engn in ENGINES:
            self._wait(engn, s, tgt)

    def a(self, lo, n, dt=BF16):
        v = self.A[:, lo:lo + n]
        return v if dt is BF16 else v.bitcast(dt)

    def b(self, lo, n, dt=BF16):
        v = self.B[:, lo:lo + n]
        return v if dt is BF16 else v.bitcast(dt)

    def p0(self):
        d, A, B = self.d, self.A, self.B
        S = self.sem
        for col, val in ((C_EXPSC, EXP_SCALE), (C_EXPBI, EXP_BIAS), (C_INVD, 1.0 / 4096.0),
                         (C_EPS, EPS), (C_NHALF, -0.5), (C_NEG1, -1.0)):
            self.pool("memset", self.consts[:, col:col + 1], val)
        self.pool("memset", self.ones[:, :], 1.0)
        self.pool("memset", self.onesrow[:, :], 1.0)
        self.pool("memset", self.ident[:, :], 0.0)
        self.pool("affine_select", out=self.ident[:, :], in_=self.ident[:, :],
                  compare_op=ALU.not_equal, fill=1.0, base=0,
                  pattern=[[-1, P]], channel_multiplier=1)
        self.pool("memset", self.masks[:, :], 1.0)
        for j in range(4):
            t_pool_const = self.pool(
                "affine_select", out=self.masks[:, j * 512:(j + 1) * 512],
                in_=self.masks[:, j * 512:(j + 1) * 512],
                compare_op=ALU.is_ge, fill=0.0, base=-128 * j,
                pattern=[[1, 512]], channel_multiplier=-1)
        self._t_pool_const = t_pool_const
        self.dma("gpsimd", self.cos[0:64, :], d["cosq"], sem=S["ldc"])
        self._t_cossin = self.dma("gpsimd", self.sin[0:64, :], d["sinq"], sem=S["ldc"])
        self.load_act_set(6)

        for src, dst, half in ((d["zg"], d["s_zsg"], 0), (d["sg"], d["s_zsg"], 1),
                               (d["zu"], d["s_zsu"], 0), (d["su"], d["s_zsu"], 1)):
            self.dma("gpsimd",
                     dst.rearrange("hc (half g j) -> g hc half j", half=2, j=128)[:, :, half, :],
                     src.rearrange("g (hc j) -> g hc j", j=128), sem=S["ldc"])
        for src, dst, half in ((d["zd"], d["s_zsd"], 0), (d["sd"], d["s_zsd"], 1)):
            self.dma("gpsimd",
                     dst.rearrange("hc (half j) -> hc half j", half=2)[:, half, :],
                     src, sem=S["ldc"], max_dma_last_dim=8192)

        xbf = [B[:, 24576:28672], B[:, 28672:32768]]
        Ar = A.rearrange("p (kc t) -> p kc t", t=T)
        psbf = [self.psA_bf, self.psB_bf]
        xbf_free = [(S["pe"].h, 0), (S["pe"].h, 0)]
        ps_free = [(S["dve"].h, 0), (S["dve"].h, 0)]
        ldp = [S["ld3"], S["ld4"]]
        for tt in range(NTT):
            par = tt % 2
            wl = [xbf_free[par]]
            t_ld = self.dma("gpsimd", xbf[par],
                            d["x"][tt * 128:(tt + 1) * 128, :],
                            waits=wl, sem=ldp[par], max_dma_last_dim=8192)
            last_tr = None
            for g8 in range(8):
                pp = g8 % 2
                trw = [(ldp[par].h, t_ld), ps_free[pp]]
                if tt == 0 and g8 == 0:
                    trw.append((S["pool"].h, self._t_pool_const))
                for j in range(4):
                    kc = 4 * g8 + j
                    last_tr = self.tr(psbf[pp][:, j * 128:(j + 1) * 128],
                                      xbf[par][:, kc * 128:(kc + 1) * 128],
                                      waits=(trw if j == 0 else ()), inc=(j == 3))
                t_ev = self.dve(
                    "tensor_copy",
                    Ar[:, 4 * g8:4 * g8 + 4, tt * 128:(tt + 1) * 128],
                    psbf[pp][:, 0:512].rearrange("p (j m) -> p j m", m=128),
                    waits=self.w((S["pe"], last_tr)))
                ps_free[pp] = (S["dve"].h, t_ev)
            xbf_free[par] = (S["pe"].h, last_tr)
        for q in range(3):
            self._p1_issue_half(q)
        self._p1_next_half = 3
        t_all_ev = S["dve"].c
        for kc in range(KC):
            self.dma("sync", d["s_xT"][kc], A[:, kc * T:(kc + 1) * T],
                     waits=(self.w((S["dve"], t_all_ev)) if kc == 0 else ()),
                     sem=S["st"])
        self.barrier()

    P1_SPECS = None

    def _p1_issue_half(self, q):
        S = self.sem
        idx, h = q // 2, q % 2
        kind, w_ap, mc = self._p1_specs[idx]
        wr = w_ap.rearrange("(kc p) m -> p kc m", p=P)
        dst = self._p1_pf[q % 3].rearrange("p (kc m) -> p kc m", m=128)
        wl = (self.w((S["dve"], self._p1_cast_tok[q - 3]))
              if (q - 3) in self._p1_cast_tok else ())
        t0 = self.dma("sync", dst[:, 0:8, :],
                      wr[:, 16 * h:16 * h + 8, mc * 128:(mc + 1) * 128],
                      waits=wl, sem=S["ld"])
        t1 = self.dma("scalar", dst[:, 8:16, :],
                      wr[:, 16 * h + 8:16 * h + 16, mc * 128:(mc + 1) * 128],
                      waits=wl, sem=S["ldb"])
        self._p1_ld_tok[q] = (t0, t1)

    def p1(self):
        d, A, B = self.d, self.A, self.B
        S = self.sem
        pbf = [B[:, 12288:16384], B[:, 16384:20480]]
        vo = B[:, 20480:22528]
        ro = B[:, 22528:24576]
        tmpA = B[0:64, 24576:26624]
        vob = B[0:64, 26624:28672]
        vst = [B[:, 28672:29184], B[:, 29184:29696]]
        psum2 = [self.psA, self.psB]
        specs = self._p1_specs
        NSP = len(specs)

        cast_tok = self._p1_cast_tok
        t_evac = [0, 0]
        mm_last = {}
        tr_done = 0
        vst_tok = [0, 0]
        vst_st = [0, 0]
        t_ro_dma = 0
        t_rope_up = 0
        t_upd_prev = 0
        t_vob = 0
        t_sv = 0
        for idx, (kind, w_ap, mc) in enumerate(specs):
            par = idx % 2
            ps = 0 if kind == "v" else idx % 2
            pan_r = pbf[par].rearrange("p (kc m) -> p kc m", m=128)
            last = None
            for h in range(2):
                q = 2 * idx + h
                t0, t1 = self._p1_ld_tok[q]
                cw = self.w((S["ld"], t0), (S["ldb"], t1))
                if (idx - 2) in mm_last:
                    cw += self.w((S["pe"], mm_last[idx - 2]))
                cast_tok[q] = self.dve("tensor_copy",
                                       pbf[par][:, h * 2048:(h + 1) * 2048],
                                       self._p1_pf[q % 3], waits=cw)
                if self._p1_next_half < 2 * NSP:
                    self._p1_issue_half(self._p1_next_half)
                    self._p1_next_half += 1
                for kc in range(16 * h, 16 * h + 16):
                    for tb in range(NTB):
                        wl = []
                        if kc == 16 * h and tb == 0:
                            wl = self.w((S["dve"], cast_tok[q]))
                            if h == 0:
                                wl += self.w((S["dve"], t_evac[ps]))
                        last = self.mm(psum2[ps][:, tb * 512:(tb + 1) * 512],
                                       pan_r[:, kc, :],
                                       A[:, kc * T + tb * 512: kc * T + (tb + 1) * 512],
                                       start=(kc == 0), stop=(kc == KC - 1),
                                       waits=wl,
                                       inc=(kc == 16 * h + 15 and tb == NTB - 1))
            mm_last[idx] = last
            if kind == "v":
                t_ev = self.dve("tensor_copy", vo, psum2[0][:, :],
                                waits=self.w((S["pe"], last), (S["pe"], tr_done)))
                for g4 in range(4):
                    pp = g4 % 2
                    pslo = 1024 * pp
                    for j in range(4):
                        tt = 4 * g4 + j
                        trw = self.w((S["dve"], vst_tok[pp])) if j == 0 else []
                        if g4 == 0 and j == 0:
                            trw += self.w((S["dve"], t_ev))
                        tr_done = self.tr(
                            self.psB_bf[:, pslo + j * 128:pslo + (j + 1) * 128],
                            vo[:, tt * 128:(tt + 1) * 128],
                            waits=trw, inc=(j == 3))
                    t_sv = self.dve("tensor_copy", vst[pp],
                                    self.psB_bf[:, pslo:pslo + 512],
                                    waits=self.w((S["pe"], tr_done),
                                                 (S["st"], vst_st[pp])))
                    vst_tok[pp] = t_sv
                    vst_st[pp] = self.dma(
                        "gpsimd",
                        d["s_v"][4 * g4:4 * g4 + 4, :, mc * 128:(mc + 1) * 128]
                        .rearrange("tt p m -> p tt m"),
                        vst[pp].rearrange("p (tt m) -> p tt m", m=128),
                        waits=self.w((S["dve"], t_sv)), sem=S["st"])
                t_evac[0] = t_ev
                t_evac[1] = t_sv
            else:
                t_ev = self.dve("tensor_copy", vo, psum2[ps][:, :],
                                waits=self.w((S["pe"], last), (S["ld5"], t_vob)))
                k0 = vo[0:64, :]
                c, s = self.cos[0:64, :], self.sin[0:64, :]
                t_vob = self.dma("gpsimd", vob, vo[64:128, :],
                                 waits=self.w((S["dve"], t_ev),
                                              (S["dve"], t_rope_up)), sem=S["ld5"])
                self.dve("tensor_tensor", tmpA, k0, c, ALU.mult,
                         waits=self.w((S["ld5"], t_upd_prev),
                                      (S["ldc"], self._t_cossin)))
                self.dve("tensor_tensor", ro[0:64, :], vob, s, ALU.mult,
                         waits=self.w((S["ld5"], t_vob), (S["stb"], t_ro_dma)))
                self.dve("tensor_tensor", ro[0:64, :], tmpA, ro[0:64, :],
                         ALU.subtract)
                self.dve("tensor_tensor", tmpA, k0, s, ALU.mult)
                self.dve("tensor_tensor", vob, vob, c, ALU.mult)
                t_rope_up = self.dve("tensor_tensor", tmpA, tmpA, vob, ALU.add)
                t_upd_prev = self.dma("gpsimd", ro[64:128, :], tmpA,
                                      waits=self.w((S["dve"], t_rope_up)), sem=S["ld5"])
                dst = d["s_k"][mc] if kind == "k" else d["s_q"][mc]
                t_ro_dma = self.dma("scalar", dst, ro,
                                    waits=self.w((S["ld5"], t_upd_prev)), sem=S["stb"])
                t_evac[ps] = t_ev
        self.barrier()

    def _p45_stream(self):
        d, A = self.d, self.A
        S = self.sem
        qd = [A[:, 24576:32768].bitcast(I32), A[:, 32768:40960].bitcast(I32)]
        qd16 = [A[:, 24576:32768].bitcast(mybir.dt.int16)
                .rearrange("p (n two) -> p n two", two=2)[:, :, 0],
                A[:, 32768:40960].bitcast(mybir.dt.int16)
                .rearrange("p (n two) -> p n two", two=2)[:, :, 0]]
        zsr = [A[:, 40960:49152], A[:, 49152:57344]]
        wdb = [A[:, 57344:61440], A[:, 61440:65536]]
        t_tt1 = [0, 0]
        t_tt2 = [0, 0]
        wdb_st = [0, 0]
        for hc in range(HC):
            par = hc % 2
            t_q = self.dma("sync", qd[par], d["qwd"][hc * 128:(hc + 1) * 128, :],
                           waits=self.w((S["pool"], t_tt1[par])), sem=S["ldb"])
            t_zs = self.dma("sync", zsr[par],
                            d["s_zsd"][hc:hc + 1, :].rearrange("o t -> (o t)").partition_broadcast(P),
                            waits=self.w((S["pool"], t_tt2[par])), sem=S["ldb"])
            t_tt1[par] = self.pool("tensor_tensor", wdb[par], qd16[par],
                                   zsr[par][:, 0:4096], ALU.subtract,
                                   waits=self.w((S["ldb"], max(t_q, t_zs)),
                                                (S["ldb"], wdb_st[par])))
            t_tt2[par] = self.pool("tensor_tensor", wdb[par], wdb[par],
                                   zsr[par][:, 4096:8192], ALU.mult)
            wdb_st[par] = self.dma("sync", d["s_wd"][hc], wdb[par],
                                   waits=self.w((S["pool"], t_tt2[par])), sem=S["ldb"])

    def p2(self):
        d, A, B = self.d, self.A, self.B
        S = self.sem
        if "mdump" in DBG_SKIP:
            self._mdump(2)
        self._p45_stream()
        for kv in range(NKV):
            self.dma("sync", B[:, kv * T:(kv + 1) * T], d["s_k"][kv])
        for tt in range(NTT):
            self.dma("sync", B[:, 16384 + tt * DKV:16384 + (tt + 1) * DKV], d["s_v"][tt])
        t_kv = S["ld"].c
        qt_r = A[:, 0:16384].rearrange("p (mc n) -> p mc n", n=512)
        et = [A[:, 16384:17408], A[:, 17408:18432]]
        ctxo = [A[:, 18432 + i * 512:18432 + (i + 1) * 512] for i in range(4)]
        lsum = A[0:1, 20480:21504].bitcast(F32)
        rsum = [A[0:1, 21504:22016], A[0:1, 22016:22528]]
        ps_sc = [self.psA[:, 0:1024], self.psB[:, 0:1024]]
        ps_ctx = [self.psA[:, 1024:1536], self.psB[:, 1024:1536]]
        ps_sum = [self.psA[0:1, 1536:2048], self.psB[0:1, 1536:2048]]

        sc_free = [0, 0]
        et_free = [0, 0]
        t_ctx_evac = [0, 0]
        t_ln = [0, 0]
        rsum_free = [0, 0]
        ctxo_st = [0] * 4
        stp = [S["st"], S["stb"]]
        t_qt_gate = (S["ld"].h, t_kv)
        pending_tail = [None]

        for qt in range(NTB):
            sq = d["s_q"].rearrange("mc p t -> p mc t")
            qw = [t_qt_gate, (S["pe"].h, S["pe"].c)]
            for g in range(4):
                t_q = self.dma("sync", qt_r[:, g * 8:(g + 1) * 8, :],
                               sq[:, g * 8:(g + 1) * 8, qt * 512:(qt + 1) * 512],
                               waits=qw)
                qw = ()
            nk = 4 * qt + 4
            npair = nk // 2
            for h in range(NH):
                kv = h // 4
                cp = h % 2
                ready = {}
                av_last = None
                sum_last = None

                def emit_sc(jp):
                    toks = []
                    for half in range(2):
                        j = 2 * jp + half
                        wl = []
                        if half == 0:
                            wl = self.w((S["act"], sc_free[jp % 2]))
                            if jp == 0:
                                wl += self.w((S["ld"], t_q))
                        toks.append(self.mm(
                            ps_sc[jp % 2][:, half * 512:(half + 1) * 512],
                            B[:, kv * T + j * 128:kv * T + (j + 1) * 128],
                            qt_r[:, h, :], start=True, stop=True,
                            waits=wl, inc=True))
                    return toks[-1]

                def emit_exp(jp, t_sc):
                    wl = self.w((S["pe"], t_sc), (S["pe"], et_free[jp % 2]))
                    tok = self.act("activation", et[jp % 2], ps_sc[jp % 2], AF.Exp,
                                   bias=self.consts[:, C_EXPBI:C_EXPBI + 1],
                                   scale=self.consts[:, C_EXPSC:C_EXPSC + 1],
                                   waits=wl)
                    sc_free[jp % 2] = tok
                    if 2 * jp >= 4 * qt:
                        mo = (2 * jp - 4 * qt) * 512
                        mt = self.dve("tensor_tensor", et[jp % 2], et[jp % 2],
                                      self.masks[:, mo:mo + 1024],
                                      ALU.mult, waits=self.w((S["act"], tok)))
                        ready[jp] = (S["dve"].h, mt)
                    else:
                        ready[jp] = (S["act"].h, tok)

                def emit_av_sum(jp):
                    nonlocal av_last, sum_last
                    for half in range(2):
                        j = 2 * jp + half
                        wl = [ready[jp]] if half == 0 else []
                        if j == 0:
                            wl += self.w((S["dve"], t_ctx_evac[cp]))
                        av_last = self.mm(
                            ps_ctx[cp],
                            B[:, 16384 + j * DKV + kv * 128:16384 + j * DKV + (kv + 1) * 128],
                            et[jp % 2][:, half * 512:(half + 1) * 512],
                            start=(j == 0), stop=(j == nk - 1),
                            waits=wl, inc=False)
                        wl2 = self.w((S["act"], t_ln[cp])) if j == 0 else ()
                        sum_last = self.mm(ps_sum[cp], self.ones[:, 0:1],
                                           et[jp % 2][:, half * 512:(half + 1) * 512],
                                           start=(j == 0), stop=(j == nk - 1),
                                           waits=wl2, inc=True)
                    et_free[jp % 2] = sum_last

                t_sc = {0: emit_sc(0)}
                emit_exp(0, t_sc[0])
                if pending_tail[0] is not None:
                    pending_tail[0]()
                    pending_tail[0] = None
                for jp in range(1, npair):
                    t_sc[jp] = emit_sc(jp)
                    emit_exp(jp, t_sc[jp])
                    emit_av_sum(jp - 1)
                emit_av_sum(npair - 1)

                t_l = self.act("activation", lsum, ps_sum[cp], AF.Ln,
                               waits=self.w((S["pe"], sum_last)))
                self.act("activation", rsum[cp], lsum, AF.Exp,
                         scale=self.consts[0:1, C_NEG1:C_NEG1 + 1],
                         waits=self.w((stp[cp], rsum_free[cp])))
                t_ln[cp] = t_l
                rsum_free[cp] = self.dma(
                    "scalar", d["s_recip"][h:h + 1, qt * 512:(qt + 1) * 512],
                    rsum[cp], sem=stp[cp])

                def mk_tail(h=h, cp=cp, qt=qt, sum_last=sum_last):
                    def f():
                        hs = h % 4
                        tok = self.dve("tensor_copy", ctxo[hs], ps_ctx[cp],
                                       waits=self.w((S["pe"], sum_last),
                                                    (stp[hs % 2], ctxo_st[hs])))
                        t_ctx_evac[cp] = tok
                        ctxo_st[hs] = self.dma(
                            "scalar", d["s_ctx"][h][:, qt * 512:(qt + 1) * 512],
                            ctxo[hs], waits=self.w((S["dve"], tok)), sem=stp[hs % 2])
                    return f
                pending_tail[0] = mk_tail()
            pending_tail[0]()
            pending_tail[0] = None
            t_qt_gate = (stp[(NH - 1) % 2].h, ctxo_st[(NH - 1) % 4])
        if "p2et" in DBG_SKIP:
            dmp = self.A[:, 20480:21504].bitcast(F32)
            seq = [("et0", et[0]), ("et1", et[1]), ("sc0", ps_sc[0]),
                   ("sc1", ps_sc[1]), ("consts", self.consts[:, 0:16]),
                   ("masks", self.masks[:, 0:512]), ("ones", self.onesrow[:, :])]
            for i, (nm, ap) in enumerate(seq):
                wl = self.w((S["ldc"], S["ldc"].c))
                if i == 0:
                    wl += self.w((stp[0], ctxo_st[0]))
                np_, nf = ap.shape[0], ap.shape[1]
                dv = dmp[0:np_, 0:nf]
                td = self.dve("tensor_copy", dv, ap, waits=wl)
                self.dma("sync", d["s_att"][8 + i][0:np_, 0:nf], dv,
                         waits=self.w((S["dve"], td)), sem=S["ldc"])
            td = self.dve("tensor_copy", dmp[0:1, :], ps_sum,
                          waits=self.w((S["ldc"], S["ldc"].c)))
            self.dma("sync", d["s_att"][5][0:1, :], dmp[0:1, :],
                     waits=self.w((S["dve"], td)), sem=S["ldc"])
            td = self.dve("tensor_copy", dmp[0:1, :], rsum,
                          waits=self.w((S["ldc"], S["ldc"].c)))
            self.dma("sync", d["s_att"][6][0:1, :], dmp[0:1, :],
                     waits=self.w((S["dve"], td)), sem=S["ldc"])
        self.barrier()

    def p3(self):
        d, A, B = self.d, self.A, self.B
        S = self.sem
        pf = [B[:, 0:4096].bitcast(F32), B[:, 4096:8192].bitcast(F32)]
        pbf = [B[:, 8192:12288], B[:, 12288:16384]]
        xoc = B[:, 16384:18432]
        xres_pair = [B[:, 18432:20480], B[:, 20480:22528]]
        sqb = [B[:, 22528:23040], B[:, 23040:23552]]
        statsf = B[:, 23552:27648].bitcast(F32)
        recip_rep = B[:, 27648:29696]
        rowsf = B[0:1, 0:12288].bitcast(F32)
        arow = B[0:1, 12288:14336]
        brow = B[0:1, 14336:16384]
        wor = d["wo"].rearrange("(kc p) m -> p kc m", p=P)

        ld_tok = {}
        cast_tok = {}

        def issue_half(q):
            idx, h = q // 2, q % 2
            dst = pf[q % 2].rearrange("p (kc m) -> p kc m", m=128)
            wl = self.w((S["dve"], cast_tok[q - 2])) if (q - 2) in cast_tok else ()
            t0 = self.dma("sync", dst[:, 0:8, :],
                          wor[:, 16 * h:16 * h + 8, idx * 128:(idx + 1) * 128],
                          waits=wl, sem=S["ld"])
            t1 = self.dma("scalar", dst[:, 8:16, :],
                          wor[:, 16 * h + 8:16 * h + 16, idx * 128:(idx + 1) * 128],
                          waits=wl, sem=S["ldb"])
            ld_tok[q] = (t0, t1)

        issue_half(0)
        issue_half(1)
        next_half = [2]
        t_rn = {}
        t_prev = 0
        for kc in range(KC):
            self.dma("gpsimd", A[:, kc * T:(kc + 1) * T], d["s_ctx"][kc], sem=S["ldc"])
            t_r = self.dma("gpsimd", recip_rep,
                           d["s_recip"][kc:kc + 1, :].rearrange("o t -> (o t)")
                           .partition_broadcast(P),
                           waits=(self.w((S["pool"], t_prev)) if kc else ()),
                           sem=S["ldc"])
            t_prev = self.pool("tensor_tensor", A[:, kc * T:(kc + 1) * T],
                               A[:, kc * T:(kc + 1) * T], recip_rep, ALU.mult,
                               waits=self.w((S["ldc"], t_r)))
            t_rn[kc] = t_prev

        st_ = {"evac": [0] * NTB, "summ": [0] * NTB, "sq_last": 0, "xres_st": [0, 0]}
        pending = [None]
        mm_last = {}
        t_xoc = 0
        for oc in range(KC):
            par = oc % 2
            pan_r = pbf[par].rearrange("p (kc m) -> p kc m", m=128)
            for h in range(2):
                q = 2 * oc + h
                t0, t1 = ld_tok[q]
                cw = self.w((S["ld"], t0), (S["ldb"], t1))
                if (oc - 2) in mm_last:
                    cw += self.w((S["pe"], mm_last[oc - 2]))
                cast_tok[q] = self.dve("tensor_copy",
                                       pbf[par][:, h * 2048:(h + 1) * 2048],
                                       pf[q % 2], waits=cw)
                if next_half[0] < 2 * KC:
                    issue_half(next_half[0])
                    next_half[0] += 1
            t_xoc = self.dma("gpsimd", xoc, d["s_xT"][oc],
                             waits=(self.w((S["dve"], st_["evac"][NTB - 1]))
                                    if oc > 0 else ()), sem=S["ldc"])
            mm_toks = [None] * NTB
            te_list = [None] * NTB
            for hp in range(2):
                last = None
                for kc in range(KC):
                    for tb in (2 * hp, 2 * hp + 1):
                        wl = []
                        if kc == 0 and tb == 2 * hp:
                            wl = self.w((S["dve"], cast_tok[2 * oc]),
                                        (S["dve"], st_["evac"][2 * hp + 1]))
                        elif kc == 16 and tb == 2 * hp:
                            wl = self.w((S["dve"], cast_tok[2 * oc + 1]))
                        if oc == 0:
                            wl += self.w((S["pool"], t_rn[kc]))
                        last = self.mm(self.psA[:, tb * 512:(tb + 1) * 512],
                                       pan_r[:, kc, :],
                                       A[:, kc * T + tb * 512:kc * T + (tb + 1) * 512],
                                       start=(kc == 0), stop=(kc == KC - 1),
                                       waits=wl, inc=(kc == KC - 1))
                        mm_toks[tb] = last
                for tb in (2 * hp, 2 * hp + 1):
                    te = self.dve("tensor_tensor",
                                  xres_pair[par][:, tb * 512:(tb + 1) * 512],
                                  self.psA[:, tb * 512:(tb + 1) * 512],
                                  xoc[:, tb * 512:(tb + 1) * 512], ALU.add,
                                  waits=self.w((S["pe"], mm_toks[tb]),
                                               (S["ldc"], t_xoc),
                                               (S["pe"], st_["summ"][tb]),
                                               (S["st"], st_["xres_st"][par])))
                    st_["evac"][tb] = te
                    te_list[tb] = te
                if hp == 0 and pending[0] is not None:
                    pending[0]()
                    pending[0] = None
            mm_last[oc] = mm_toks[NTB - 1]

            def mk(oc=oc, te_list=te_list, xres=xres_pair[par]):
                def f():
                    for tb in range(NTB):
                        tq = self.dve("tensor_tensor", sqb[tb % 2],
                                      xres[:, tb * 512:(tb + 1) * 512],
                                      xres[:, tb * 512:(tb + 1) * 512], ALU.mult,
                                      waits=self.w((S["dve"], te_list[tb]),
                                                   (S["pe"], st_["sq_last"])))
                        st_["summ"][tb] = self.mm(
                            self.psB[0:1, tb * 512:(tb + 1) * 512],
                            self.ones[:, 0:1], xres[:, tb * 512:(tb + 1) * 512],
                            start=(oc == 0), stop=(oc == KC - 1),
                            waits=self.w((S["dve"], te_list[tb])), inc=True)
                        st_["sq_last"] = self.mm(
                            self.psB[32:33, tb * 512:(tb + 1) * 512],
                            self.ones[:, 0:1], sqb[tb % 2],
                            start=(oc == 0), stop=(oc == KC - 1),
                            waits=self.w((S["dve"], tq)), inc=True)
                return f
            pending[0] = mk()
            if oc == KC - 1:
                pending[0]()
                pending[0] = None
            st_["xres_st"][par] = self.dma(
                "gpsimd", d["s_xres"][oc], xres_pair[par],
                waits=self.w((S["dve"], st_["evac"][NTB - 1])), sem=S["st"])
        t_st = self.dve("tensor_copy", statsf[0:64, :], self.psB[0:64, :],
                        waits=self.w((S["pe"], st_["sq_last"])))
        sums, sumsq = statsf[0:1, :], statsf[32:33, :]
        m, ex2 = rowsf[:, 0:2048], rowsf[:, 2048:4096]
        tmp = rowsf[:, 4096:6144]
        c = self.consts
        self.dve("tensor_scalar", m, sums, c[0:1, C_INVD:C_INVD + 1], None, ALU.mult)
        self.dve("tensor_scalar", ex2, sumsq, c[0:1, C_INVD:C_INVD + 1], None, ALU.mult)
        self.dve("tensor_tensor", tmp, m, m, ALU.mult)
        t_var = self.dve("tensor_tensor", ex2, ex2, tmp, ALU.subtract)
        t_l = self.act("activation", tmp, ex2, AF.Ln,
                       bias=c[0:1, C_EPS:C_EPS + 1], waits=self.w((S["dve"], t_var)))
        t_r = self.act("activation", ex2, tmp, AF.Exp,
                       scale=c[0:1, C_NHALF:C_NHALF + 1])
        t_a = self.dve("tensor_copy", arow, ex2, waits=self.w((S["act"], t_r)))
        self.dve("tensor_tensor", tmp, m, ex2, ALU.mult)
        t_b = self.dve("tensor_scalar", brow, tmp, c[0:1, C_NEG1:C_NEG1 + 1], None,
                       ALU.mult)
        self.dma("gpsimd", d["s_ab"][0:1, :], arow, waits=self.w((S["dve"], t_a)),
                 sem=S["st"])
        self.dma("gpsimd", d["s_ab"][1:2, :], brow, waits=self.w((S["dve"], t_b)),
                 sem=S["st"])
        self.barrier()

    def p4pre(self):
        d, A, B = self.d, self.A, self.B
        S = self.sem
        a_rep, b_rep = B[:, 0:2048], B[:, 2048:4096]
        t_a = self.dma("sync", a_rep, d["s_ab"][0:1, :].rearrange("o t -> (o t)").partition_broadcast(P))
        t_b = self.dma("sync", b_rep, d["s_ab"][1:2, :].rearrange("o t -> (o t)").partition_broadcast(P))
        t_lds = [self.dma("sync", A[:, kc * T:(kc + 1) * T], d["s_xres"][kc])
                 for kc in range(KC)]
        for kc in range(KC):
            sl = A[:, kc * T:(kc + 1) * T]
            self.dve("tensor_tensor", sl, sl, a_rep, ALU.mult,
                     waits=self.w((S["ld"], t_lds[-1])))
            tx = self.dve("tensor_tensor", sl, sl, b_rep, ALU.add)
            if "x1" in self.debug:
                self.dma("sync", d["s_x1"][kc], sl, waits=self.w((S["dve"], tx)),
                         sem=S["st"])
        self.barrier()

    def p4(self):
        d, A, B = self.d, self.A, self.B
        S = self.sem
        self.load_act_set(18)
        qbuf = B[:, 0:8192].bitcast(I32)
        qbuf_r = qbuf.rearrange("p (kc m) -> p kc m", m=128)
        qbuf_i16 = B[:, 0:8192].bitcast(mybir.dt.int16) \
            .rearrange("p (n two) -> p n two", two=2)[:, :, 0]
        zpart = B[:, 8192:12288]
        spart = B[:, 12288:16384]
        zs = B[:, 8192:16384]
        wdeq = {"g": B[:, 16384:20480], "u": B[:, 20480:24576]}
        ho = [B[:, 24576:26624], B[:, 26624:28672]]
        sg = B[:, 28672:30720]
        qws = {"g": d["qwg"].rearrange("(kc p) n -> p kc n", p=P),
               "u": d["qwu"].rearrange("(kc p) n -> p kc n", p=P)}
        zss = {"g": d["s_zsg"], "u": d["s_zsu"]}
        psm = {"g": self.psA, "u": self.psB}

        t_tt1 = 0
        t_tt2 = 0
        wdeq_free = {"g": 0, "u": 0}
        silu_done = [0] * NTB
        tth_done = [0] * NTB
        ho_st = [0, 0]
        stp = [S["st"], S["stb"]]
        for hc in range(HC):
            mm_tok = {}
            for m in ("g", "u"):
                ldt = []
                for hh in range(2):
                    ldt.append(self.dma(
                        "sync", qbuf_r[:, 16 * hh:16 * hh + 16, :],
                        qws[m][:, 16 * hh:16 * hh + 16, hc * 128:(hc + 1) * 128],
                        waits=(self.w((S["pool"], t_tt1)) if hh == 0 else ())))
                t_zs = self.dma("sync", zs,
                                zss[m][hc:hc + 1, :].rearrange("o t -> (o t)").partition_broadcast(P),
                                waits=self.w((S["pool"], t_tt2)))
                t_tt1 = self.pool("tensor_tensor", wdeq[m], qbuf_i16, zpart, ALU.subtract,
                                  waits=self.w((S["ld"], max(ldt[-1], t_zs)),
                                               (S["pe"], wdeq_free[m])))
                t_tt2 = self.pool("tensor_tensor", wdeq[m], wdeq[m], spart, ALU.mult)
                toks = []
                for tb in range(NTB):
                    last = None
                    for kc in range(KC):
                        wl = []
                        if kc == 0:
                            wl = self.w((S["pool"], t_tt2))
                            if m == "g":
                                wl += self.w((S["act"], silu_done[tb]))
                            else:
                                wl += self.w((S["dve"], tth_done[tb]))
                        last = self.mm(psm[m][:, tb * 512:(tb + 1) * 512],
                                       wdeq[m][:, kc * 128:(kc + 1) * 128],
                                       A[:, kc * T + tb * 512:kc * T + (tb + 1) * 512],
                                       start=(kc == 0), stop=(kc == KC - 1),
                                       waits=wl, inc=(kc == KC - 1))
                    toks.append(last)
                mm_tok[m] = toks
                wdeq_free[m] = toks[-1]
            par = hc % 2
            tth = []
            for tb in range(NTB):
                ts = self.act("activation", sg[:, tb * 512:(tb + 1) * 512],
                              self.psA[:, tb * 512:(tb + 1) * 512], AF.Silu,
                              waits=self.w((S["pe"], mm_tok["g"][tb]),
                                           (S["dve"], tth_done[tb])))
                silu_done[tb] = ts
                th = self.dve("tensor_tensor", ho[par][:, tb * 512:(tb + 1) * 512],
                              self.psB[:, tb * 512:(tb + 1) * 512],
                              sg[:, tb * 512:(tb + 1) * 512], ALU.mult,
                              waits=self.w((S["pe"], mm_tok["u"][tb]),
                                           (S["act"], ts), (stp[par], ho_st[par])))
                tth_done[tb] = th
                tth.append(th)
            ho_st[par] = self.dma("scalar", d["s_h"][hc], ho[par],
                                  waits=self.w((S["dve"], tth[-1])), sem=stp[par])
        self.barrier()

    def p45(self):
        d, A, B = self.d, self.A, self.B
        S = self.sem
        qd = B[:, 0:8192].bitcast(I32)
        zpart = B[:, 8192:12288]
        spart = B[:, 12288:16384]
        zs = B[:, 8192:16384]
        wdb = [B[:, 16384:20480], B[:, 20480:24576]]
        t_tt1 = 0
        t_tt2 = 0
        wdb_st = [0, 0]
        stp = [S["st"], S["stb"]]
        for hc in range(HC):
            par = hc % 2
            t_q = self.dma("sync", qd, d["qwd"][hc * 128:(hc + 1) * 128, :],
                           waits=self.w((S["dve"], t_tt1)))
            t_zs = self.dma("sync", zs, d["s_zsd"][hc:hc + 1, :].rearrange("o t -> (o t)").partition_broadcast(P),
                            waits=self.w((S["dve"], t_tt2)))
            t_tt1 = self.dve("tensor_tensor", wdb[par], qd, zpart, ALU.subtract,
                             waits=self.w((S["ld"], max(t_q, t_zs)),
                                          (stp[par], wdb_st[par])))
            t_tt2 = self.dve("tensor_tensor", wdb[par], wdb[par], spart, ALU.mult)
            wdb_st[par] = self.dma("sync", d["s_wd"][hc], wdb[par],
                                   waits=self.w((S["dve"], t_tt2)), sem=stp[par])
        self.barrier()

    def p5(self):
        d, A, B = self.d, self.A, self.B
        S = self.sem
        NR = 8
        ht = [B[:, i * 1024:(i + 1) * 1024] for i in range(NR)]
        wt = [B[:, 8192 + i * 512:8192 + (i + 1) * 512] for i in range(NR)]
        psY = [self.psA[:, 0:1024], self.psA[:, 1024:2048],
               self.psB[:, 0:1024], self.psB[:, 1024:2048]]
        buf_free = [0] * NR
        t_ev = [0] * 4
        ldp = [S["ld"], S["ldb"], S["ldc"], S["ld3"], S["ld4"], S["ld5"], S["ld6"], S["ld7"]]
        for dq in range(8):
            for th in range(2):
                hc_tok = {}
                for hc in range(HC):
                    par = hc % NR
                    t_h = self.dma("sync", ht[par],
                                   d["s_h"][hc][:, th * 1024:(th + 1) * 1024],
                                   waits=self.w((S["pe"], buf_free[par])),
                                   sem=ldp[par])
                    t_w = self.dma("scalar", wt[par],
                                   d["s_wd"][hc][:, dq * 512:(dq + 1) * 512],
                                   waits=self.w((S["pe"], buf_free[par])),
                                   sem=ldp[par])
                    last = None
                    for dc in range(4):
                        for tb2 in range(2):
                            wl = []
                            if dc == 0 and tb2 == 0:
                                wl = self.w((ldp[par], max(t_h, t_w)))
                                if hc == 0:
                                    wl += self.w((S["dve"], max(t_ev)))
                            last = self.mm(
                                psY[dc][:, tb2 * 512:(tb2 + 1) * 512],
                                wt[par][:, dc * 128:(dc + 1) * 128],
                                ht[par][:, tb2 * 512:(tb2 + 1) * 512],
                                start=(hc == 0), stop=(hc == HC - 1),
                                waits=wl, inc=(dc == 3 and tb2 == 1))
                    buf_free[par] = last
                    hc_tok[hc] = last
                for dc in range(4):
                    dcg = dq * 4 + dc
                    sl = A[:, dcg * T + th * 1024:dcg * T + (th + 1) * 1024]
                    t_ev[dc] = self.dve("tensor_tensor", sl, psY[dc], sl, ALU.add,
                                        waits=self.w((S["pe"], hc_tok[HC - 1])))
        if "x2" in self.debug:
            tx = S["dve"].c
            for kc in range(KC):
                self.dma("sync", d["s_x2"][kc], A[:, kc * T:(kc + 1) * T],
                         waits=self.w((S["dve"], tx)), sem=S["st"])
        self.barrier()

    def p6(self):
        d, A, B = self.d, self.A, self.B
        S = self.sem
        self.load_act_set(6)
        sq = [B[:, 3072:3584], B[:, 3584:4096]]
        statsf = B[:, 12800:16896].bitcast(F32)
        rowsf = B[0:1, 16896:29184].bitcast(F32)
        c = self.consts
        t_sqmm = [0, 0]
        t_last = None
        for tb in range(NTB):
            for kc in range(KC):
                par = kc % 2
                sl = A[:, kc * T + tb * 512:kc * T + (tb + 1) * 512]
                tq = self.dve("tensor_tensor", sq[par], sl, sl, ALU.mult,
                              waits=self.w((S["pe"], t_sqmm[par])))
                self.mm(self.psA[0:1, tb * 512:(tb + 1) * 512],
                        self.ones[:, 0:1], sl,
                        start=(kc == 0), stop=(kc == KC - 1), waits=(), inc=False)
                t_sqmm[par] = self.mm(self.psA[32:33, tb * 512:(tb + 1) * 512],
                                      self.ones[:, 0:1], sq[par],
                                      start=(kc == 0), stop=(kc == KC - 1),
                                      waits=self.w((S["dve"], tq)), inc=True)
                t_last = t_sqmm[par]
        t_stats = self.dve("tensor_copy", statsf[0:64, :], self.psA[0:64, :],
                           waits=self.w((S["pe"], t_last)))
        sums, sumsq = statsf[0:1, :], statsf[32:33, :]
        m, ex2 = rowsf[:, 0:2048], rowsf[:, 2048:4096]
        tmp = rowsf[:, 4096:6144]
        self.dve("tensor_scalar", m, sums, c[0:1, C_INVD:C_INVD + 1], None, ALU.mult)
        self.dve("tensor_scalar", ex2, sumsq, c[0:1, C_INVD:C_INVD + 1], None, ALU.mult)
        self.dve("tensor_tensor", tmp, m, m, ALU.mult)
        t_var = self.dve("tensor_tensor", ex2, ex2, tmp, ALU.subtract)
        self.act("activation", tmp, ex2, AF.Ln,
                 bias=c[0:1, C_EPS:C_EPS + 1], waits=self.w((S["dve"], t_var)))
        t_r = self.act("activation", ex2, tmp, AF.Exp,
                       scale=c[0:1, C_NHALF:C_NHALF + 1])
        t_nm = self.dve("tensor_tensor", tmp, m, ex2, ALU.mult,
                        waits=self.w((S["act"], t_r)))
        t_nm = self.dve("tensor_scalar", tmp, tmp, c[0:1, C_NEG1:C_NEG1 + 1], None,
                        ALU.mult)
        t_s1 = self.dma("sync", d["s_ln2"][0:1, :], ex2,
                        waits=self.w((S["act"], t_r)), sem=S["st"])
        t_s2 = self.dma("sync", d["s_ln2"][1:2, :], tmp,
                        waits=self.w((S["dve"], t_nm)), sem=S["st"])
        rcol = B[:, 25088:25120].bitcast(F32)
        nmrcol = B[:, 25120:25152].bitcast(F32)
        t_rc = self.dma("sync", rcol,
                        d["s_ln2"][0:1, :].rearrange("o (tt p) -> (o p) tt", p=P),
                        waits=self.w((S["st"], t_s1)),
                        sem=S["ldc"], allow_slow_non_contiguous=True)
        t_nc = self.dma("sync", nmrcol,
                        d["s_ln2"][1:2, :].rearrange("o (tt p) -> (o p) tt", p=P),
                        waits=self.w((S["st"], t_s2)),
                        sem=S["ldc"], allow_slow_non_contiguous=True)
        outb = [B[:, 4096:12288].bitcast(F32), B[:, 16896:25088].bitcast(F32)]
        psbf = [self.psA_bf, self.psB_bf]
        ps_free = [(S["dve"].h, t_stats), (S["dve"].h, t_stats)]
        out_st = [0, 0]
        stp = [S["st"], S["stb"]]
        first = self.w((S["ldc"], max(t_rc, t_nc)))
        for tt in range(NTT):
            par = tt % 2
            t_aff = []
            for kcg in range(8):
                pp = kcg % 2
                last_tr = None
                for j in range(4):
                    kc = 4 * kcg + j
                    last_tr = self.tr(psbf[pp][:, j * 128:(j + 1) * 128],
                                      A[:, kc * T + tt * 128:kc * T + (tt + 1) * 128],
                                      waits=([ps_free[pp]] if j == 0 else ()),
                                      inc=(j == 3))
                wl = first + self.w((S["pe"], last_tr), (stp[par], out_st[par]))
                first = []
                ta = self.dve("tensor_scalar",
                              outb[par][:, kcg * 512:(kcg + 1) * 512],
                              psbf[pp][:, 0:512],
                              rcol[:, tt:tt + 1], nmrcol[:, tt:tt + 1],
                              ALU.mult, ALU.add, waits=wl)
                ps_free[pp] = (S["dve"].h, ta)
                t_aff.append(ta)
            out_st[par] = self.dma("sync", d["out"][tt * 128:(tt + 1) * 128, :],
                                   outb[par], waits=self.w((S["dve"], t_aff[-1])),
                                   sem=stp[par])
        self._wait("sync", S["st"], S["st"].c)
        self._wait("sync", S["stb"], S["stb"].c)
        self.barrier()

    def finalize(self):
        nc = self.nc
        with nc.Block() as block:
            for engn in ENGINES:
                def runner(e, engn=engn):
                    for f in self.ops[engn]:
                        f(e)
                getattr(block, engn)(runner)
        self.es.close()
        return nc


def build(debug=(), nphases=9):
    b = Builder(debug)
    phases = [b.p0, b.p1, b.p2, b.p3, b.p4pre, b.p4, b.p5, b.p6]
    for p in phases[:nphases]:
        p()
    return b.finalize()


def _rope_tables():
    inv = 1.0 / (500000.0 ** (np.arange(0, 128, 2, dtype=np.float64) / 128.0))
    t = np.arange(T, dtype=np.float64)
    freqs = np.outer(t, inv)
    cos = np.cos(freqs).T.astype(np.float32)
    sin = np.sin(freqs).T.astype(np.float32)
    return np.ascontiguousarray(cos), np.ascontiguousarray(sin)


_NC_CACHE = {}


def kernel(L0_attn_x, L0_attn_Wq, L0_attn_Wk, L0_attn_Wv, L0_attn_Wo,
           L0_ffn_qw_gate, L0_ffn_z_gate, L0_ffn_s_gate,
           L0_ffn_qw_up, L0_ffn_z_up, L0_ffn_s_up,
           L0_ffn_qw_down, L0_ffn_z_down, L0_ffn_s_down):
    if "nc" not in _NC_CACHE:
        _NC_CACHE["nc"] = build()
    nc = _NC_CACHE["nc"]
    cos, sin = _rope_tables()
    common = {
        "wq": np.ascontiguousarray(L0_attn_Wq, dtype=np.float32),
        "wk": np.ascontiguousarray(L0_attn_Wk, dtype=np.float32),
        "wv": np.ascontiguousarray(L0_attn_Wv, dtype=np.float32),
        "wo": np.ascontiguousarray(L0_attn_Wo, dtype=np.float32),
        "qwg": np.ascontiguousarray(L0_ffn_qw_gate, dtype=np.int32),
        "zg": np.ascontiguousarray(L0_ffn_z_gate, dtype=np.int32),
        "sg": np.ascontiguousarray(L0_ffn_s_gate, dtype=np.float32),
        "qwu": np.ascontiguousarray(L0_ffn_qw_up, dtype=np.int32),
        "zu": np.ascontiguousarray(L0_ffn_z_up, dtype=np.int32),
        "su": np.ascontiguousarray(L0_ffn_s_up, dtype=np.float32),
        "qwd": np.ascontiguousarray(L0_ffn_qw_down, dtype=np.int32),
        "zd": np.ascontiguousarray(L0_ffn_z_down, dtype=np.int32),
        "sd": np.ascontiguousarray(L0_ffn_s_down, dtype=np.float32),
        "cosq": cos, "sinq": sin,
    }
    x = np.asarray(L0_attn_x, dtype=np.float32)
    in_maps = [dict(common, x=np.ascontiguousarray(x[i])) for i in range(8)]
    res = run_bass_kernel_spmd(nc, in_maps, core_ids=list(range(8)))
    return np.stack([res.results[i]["out"] for i in range(8)], axis=0)


# version 13
# speedup vs baseline: 1.2407x; 1.2301x over previous
import math
from contextlib import ExitStack

import numpy as np
import concourse.bass as bass
import concourse.mybir as mybir
from concourse.bass_utils import run_bass_kernel_spmd

F32 = mybir.dt.float32
BF16 = mybir.dt.bfloat16
I32 = mybir.dt.int32
AF = mybir.ActivationFunctionType
ALU = mybir.AluOpType
AX = mybir.AxisListType.X

P = 128
T = 2048
D = 4096
KC = 32
DKV = 1024
NKV = 8
NH = 32
HID = 11008
HC = 86
NTT = 16
NTB = 4
EPS = 1e-5
EXP_SCALE = 1.0 / math.sqrt(128.0)
EXP_BIAS = -20.0

ENGINES = ("sync", "tensor", "vector", "scalar", "gpsimd")
DBG_SKIP = set()

C_EXPSC = 0
C_EXPBI = 1
C_INVD = 2
C_EPS = 3
C_NHALF = 4
C_NEG1 = 5


class Sem:

    def __init__(self, h):
        self.h = h
        self.c = 0

    def bump(self, n=1):
        self.c += n
        return self.c


class Builder:
    def __init__(self, debug=()):
        self.debug = set(debug)
        nc = bass.Bass("TRN2", target_bir_lowering=False)
        self.nc = nc
        self.ops = {e: [] for e in ENGINES}
        self.es = ExitStack()

        d = {}
        d["x"] = nc.dram_tensor("x", [T, D], F32, kind="ExternalInput").ap()
        d["wq"] = nc.dram_tensor("wq", [D, D], F32, kind="ExternalInput").ap()
        d["wk"] = nc.dram_tensor("wk", [D, DKV], F32, kind="ExternalInput").ap()
        d["wv"] = nc.dram_tensor("wv", [D, DKV], F32, kind="ExternalInput").ap()
        d["wo"] = nc.dram_tensor("wo", [D, D], F32, kind="ExternalInput").ap()
        d["qwg"] = nc.dram_tensor("qwg", [D, HID], I32, kind="ExternalInput").ap()
        d["qwu"] = nc.dram_tensor("qwu", [D, HID], I32, kind="ExternalInput").ap()
        d["qwd"] = nc.dram_tensor("qwd", [HID, D], I32, kind="ExternalInput").ap()
        d["zg"] = nc.dram_tensor("zg", [32, HID], I32, kind="ExternalInput").ap()
        d["sg"] = nc.dram_tensor("sg", [32, HID], F32, kind="ExternalInput").ap()
        d["zu"] = nc.dram_tensor("zu", [32, HID], I32, kind="ExternalInput").ap()
        d["su"] = nc.dram_tensor("su", [32, HID], F32, kind="ExternalInput").ap()
        d["zd"] = nc.dram_tensor("zd", [HC, D], I32, kind="ExternalInput").ap()
        d["sd"] = nc.dram_tensor("sd", [HC, D], F32, kind="ExternalInput").ap()
        d["cosq"] = nc.dram_tensor("cosq", [64, T], F32, kind="ExternalInput").ap()
        d["sinq"] = nc.dram_tensor("sinq", [64, T], F32, kind="ExternalInput").ap()
        d["out"] = nc.dram_tensor("out", [T, D], F32, kind="ExternalOutput").ap()

        def scratch(name, shape, dtype):
            if name in self.debug and dtype is BF16:
                t = nc.dram_tensor("s_" + name, shape, mybir.dt.uint16,
                                   kind="ExternalOutput").ap()
                return t.bitcast(BF16)
            kind = "ExternalOutput" if name in self.debug else "Internal"
            return nc.dram_tensor("s_" + name, shape, dtype, kind=kind).ap()

        d["s_xT"] = scratch("xT", [KC, P, T], BF16)
        d["s_k"] = scratch("k", [NKV, P, T], BF16)
        d["s_v"] = scratch("v", [NTT, P, DKV], BF16)
        d["s_q"] = scratch("q", [NH, P, T], BF16)
        d["s_ctx"] = scratch("ctx", [NH, P, T], BF16)
        d["s_xres"] = scratch("xres", [KC, P, T], BF16)
        d["s_ab"] = scratch("ab", [2, T], BF16)
        d["s_h"] = scratch("h", [HC, P, T], BF16)
        d["s_wd"] = scratch("wd", [HC, P, D], BF16)
        d["s_zsg"] = scratch("zsg", [HC, 2 * D], BF16)
        d["s_zsu"] = scratch("zsu", [HC, 2 * D], BF16)
        d["s_zsd"] = scratch("zsd", [HC, 2 * D], BF16)
        d["s_ln2"] = scratch("ln2", [2, T], F32)
        d["s_recip"] = scratch("recip", [NH, T], BF16)
        if "att" in self.debug:
            d["s_att"] = nc.dram_tensor("s_att", [16, P, 512], F32,
                                        kind="ExternalOutput").ap()
        if "x1" in self.debug:
            d["s_x1"] = scratch("x1", [KC, P, T], BF16)
        if "x2" in self.debug:
            d["s_x2"] = scratch("x2", [KC, P, T], BF16)
        self.d = d

        sb = self.es.enter_context
        self.A = sb(nc.sbuf_tensor("arenaA", [P, 65536], BF16)).ap()
        self.B = sb(nc.sbuf_tensor("arenaB", [P, 32768], BF16)).ap()
        self.consts = sb(nc.sbuf_tensor("consts", [P, 16], F32)).ap()
        self.ones = sb(nc.sbuf_tensor("ones", [P, 8], BF16)).ap()
        self.onesrow = sb(nc.sbuf_tensor("onesrow", [1, 128], BF16)).ap()
        self.ident = sb(nc.sbuf_tensor("ident", [P, P], BF16)).ap()
        self.masks = sb(nc.sbuf_tensor("masks", [P, 4 * 512], BF16)).ap()
        self.cos = sb(nc.sbuf_tensor("cos", [P, 2048], BF16)).ap()
        self.sin = sb(nc.sbuf_tensor("sin", [P, 2048], BF16)).ap()
        self.psA = sb(nc.psum_tensor("psA", [P, 2048], F32)).ap()
        self.psB = sb(nc.psum_tensor("psB", [P, 2048], F32)).ap()
        self.psA_bf = self.psA.bitcast(BF16)
        self.psB_bf = self.psB.bitcast(BF16)

        self.sem = {}
        for name in ("ld", "ldb", "ldc", "ld3", "ld4", "ld5", "ld6", "ld7", "pe", "act", "dve", "pool", "st", "stb", "bar"):
            self.sem[name] = Sem(self.es.enter_context(nc.semaphore(name)))

        self._p1_specs = ([("v", d["wv"], mc) for mc in range(NKV)]
                          + [("k", d["wk"], mc) for mc in range(NKV)]
                          + [("q", d["wq"], mc) for mc in range(NH)])
        self._p1_pf = [self.B[:, 0:4096].bitcast(F32),
                       self.B[:, 4096:8192].bitcast(F32),
                       self.B[:, 8192:12288].bitcast(F32)]
        self._p1_cast_tok = {}
        self._p1_ld_tok = {}
        self._p1_next_half = 0
        self._t_cossin = 0

    def _op(self, engn, meth, *a, waits=(), inc=None, incn=1, **kw):
        waits = tuple(waits)

        def f(e, meth=meth, a=a, kw=kw, waits=waits, inc=inc, incn=incn):
            for (h, v) in waits:
                e.wait_ge(h, v)
            r = getattr(e, meth)(*a, **kw)
            if inc is not None:
                r.then_inc(inc, incn)

        self.ops[engn].append(f)

    def _wait(self, engn, sem, val):
        def f(e, h=sem.h, v=val):
            e.wait_ge(h, v)

        self.ops[engn].append(f)

    def w(self, *pairs):
        return [(s.h, v) for (s, v) in pairs]

    def dma(self, engn, out, in_, waits=(), sem=None, **kw):
        s = sem or self.sem["ld"]
        tok = s.bump(16)
        self._op(engn, "dma_start", waits=waits, inc=s.h, incn=16, out=out, in_=in_, **kw)
        return tok

    def mm(self, out, lhsT, rhs, start, stop, waits=(), inc=False):
        s = self.sem["pe"]
        tok = s.bump() if inc else None
        self._op("tensor", "matmul", out, lhsT, rhs, waits=waits,
                 inc=(s.h if inc else None), start=start, stop=stop)
        return tok

    def tr(self, out, in_, waits=(), inc=False):
        s = self.sem["pe"]
        tok = s.bump() if inc else None
        self._op("tensor", "transpose", out, in_, self.ident, waits=waits,
                 inc=(s.h if inc else None))
        return tok

    def _chained(self, engn, semname, meth, *a, waits=(), **kw):
        s = self.sem[semname]
        wl = list(waits) + [(s.h, s.c)]
        tok = s.bump()
        self._op(engn, meth, *a, waits=wl, inc=s.h, **kw)
        return tok

    def dve(self, meth, *a, waits=(), **kw):
        return self._chained("vector", "dve", meth, *a, waits=waits, **kw)

    def act(self, meth, *a, waits=(), **kw):
        return self._chained("scalar", "act", meth, *a, waits=waits, **kw)

    def pool(self, meth, *a, waits=(), **kw):
        return self._chained("gpsimd", "pool", meth, *a, waits=waits, **kw)

    def load_act_set(self, set_id):
        def f(e, set_id=set_id):
            e.add_instruction(mybir.InstLoadActFuncSet(
                name=self.nc.get_next_instruction_name(),
                act_func_set_id=set_id, ins=[], outs=[]))
        self.ops["scalar"].append(f)

    def barrier(self, skip=()):
        for nm in ("ld", "ldb", "ldc", "ld3", "ld4", "ld5", "ld6", "ld7", "st", "stb"):
            if nm in skip:
                continue
            sm = self.sem[nm]
            if sm.c:
                self._wait("sync", sm, sm.c)
        s = self.sem["bar"]
        base = s.c
        for engn in ENGINES:
            self._op(engn, "nop", inc=s.h)
        s.c = base + 5
        tgt = s.c
        for engn in ENGINES:
            self._wait(engn, s, tgt)

    def a(self, lo, n, dt=BF16):
        v = self.A[:, lo:lo + n]
        return v if dt is BF16 else v.bitcast(dt)

    def b(self, lo, n, dt=BF16):
        v = self.B[:, lo:lo + n]
        return v if dt is BF16 else v.bitcast(dt)

    def p0(self):
        d, A, B = self.d, self.A, self.B
        S = self.sem
        for col, val in ((C_EXPSC, EXP_SCALE), (C_EXPBI, EXP_BIAS), (C_INVD, 1.0 / 4096.0),
                         (C_EPS, EPS), (C_NHALF, -0.5), (C_NEG1, -1.0)):
            self.pool("memset", self.consts[:, col:col + 1], val)
        self.pool("memset", self.ones[:, :], 1.0)
        self.pool("memset", self.onesrow[:, :], 1.0)
        self.pool("memset", self.ident[:, :], 0.0)
        self.pool("affine_select", out=self.ident[:, :], in_=self.ident[:, :],
                  compare_op=ALU.not_equal, fill=1.0, base=0,
                  pattern=[[-1, P]], channel_multiplier=1)
        self.pool("memset", self.masks[:, :], 1.0)
        for j in range(4):
            t_pool_const = self.pool(
                "affine_select", out=self.masks[:, j * 512:(j + 1) * 512],
                in_=self.masks[:, j * 512:(j + 1) * 512],
                compare_op=ALU.is_ge, fill=0.0, base=-128 * j,
                pattern=[[1, 512]], channel_multiplier=-1)
        self._t_pool_const = t_pool_const
        self.dma("gpsimd", self.cos[0:64, :], d["cosq"], sem=S["ldc"])
        self._t_cossin = self.dma("gpsimd", self.sin[0:64, :], d["sinq"], sem=S["ldc"])
        self.load_act_set(6)

        for src, dst, half in ((d["zg"], d["s_zsg"], 0), (d["sg"], d["s_zsg"], 1),
                               (d["zu"], d["s_zsu"], 0), (d["su"], d["s_zsu"], 1)):
            self._t_zs_gu = self.dma(
                "gpsimd",
                dst.rearrange("hc (half g j) -> g hc half j", half=2, j=128)[:, :, half, :],
                src.rearrange("g (hc j) -> g hc j", j=128), sem=S["ld6"])
        for src, dst, half in ((d["zd"], d["s_zsd"], 0), (d["sd"], d["s_zsd"], 1)):
            self._t_zs_all = self.dma(
                "gpsimd",
                dst.rearrange("hc (half j) -> hc half j", half=2)[:, half, :],
                src, sem=S["ld6"], max_dma_last_dim=8192)

        xbf = [B[:, 24576:28672], B[:, 28672:32768]]
        Ar = A.rearrange("p (kc t) -> p kc t", t=T)
        psbf = [self.psA_bf, self.psB_bf]
        xbf_free = [(S["pe"].h, 0), (S["pe"].h, 0)]
        ps_free = [(S["dve"].h, 0), (S["dve"].h, 0)]
        ldp = [S["ld3"], S["ld4"]]
        for tt in range(NTT):
            par = tt % 2
            wl = [xbf_free[par]]
            t_ld = self.dma("gpsimd", xbf[par],
                            d["x"][tt * 128:(tt + 1) * 128, :],
                            waits=wl, sem=ldp[par], max_dma_last_dim=8192)
            last_tr = None
            for g8 in range(8):
                pp = g8 % 2
                trw = [(ldp[par].h, t_ld), ps_free[pp]]
                if tt == 0 and g8 == 0:
                    trw.append((S["pool"].h, self._t_pool_const))
                for j in range(4):
                    kc = 4 * g8 + j
                    last_tr = self.tr(psbf[pp][:, j * 128:(j + 1) * 128],
                                      xbf[par][:, kc * 128:(kc + 1) * 128],
                                      waits=(trw if j == 0 else ()), inc=(j == 3))
                t_ev = self.dve(
                    "tensor_copy",
                    Ar[:, 4 * g8:4 * g8 + 4, tt * 128:(tt + 1) * 128],
                    psbf[pp][:, 0:512].rearrange("p (j m) -> p j m", m=128),
                    waits=self.w((S["pe"], last_tr)))
                ps_free[pp] = (S["dve"].h, t_ev)
            xbf_free[par] = (S["pe"].h, last_tr)
        for q in range(3):
            self._p1_issue_half(q)
        self._p1_next_half = 3
        t_all_ev = S["dve"].c
        for kc in range(KC):
            self.dma("sync", d["s_xT"][kc], A[:, kc * T:(kc + 1) * T],
                     waits=(self.w((S["dve"], t_all_ev)) if kc == 0 else ()),
                     sem=S["st"])
        self.barrier(skip=("ld6",))

    P1_SPECS = None

    def _p1_issue_half(self, q):
        S = self.sem
        idx, h = q // 2, q % 2
        kind, w_ap, mc = self._p1_specs[idx]
        wr = w_ap.rearrange("(kc p) m -> p kc m", p=P)
        dst = self._p1_pf[q % 3].rearrange("p (kc m) -> p kc m", m=128)
        wl = (self.w((S["dve"], self._p1_cast_tok[q - 3]))
              if (q - 3) in self._p1_cast_tok else ())
        t0 = self.dma("sync", dst[:, 0:8, :],
                      wr[:, 16 * h:16 * h + 8, mc * 128:(mc + 1) * 128],
                      waits=wl, sem=S["ld"])
        t1 = self.dma("scalar", dst[:, 8:16, :],
                      wr[:, 16 * h + 8:16 * h + 16, mc * 128:(mc + 1) * 128],
                      waits=wl, sem=S["ldb"])
        self._p1_ld_tok[q] = (t0, t1)

    def p1(self):
        d, A, B = self.d, self.A, self.B
        S = self.sem
        pbf = [B[:, 12288:16384], B[:, 16384:20480]]
        vo = B[:, 20480:22528]
        ro = B[:, 22528:24576]
        tmpA = B[0:64, 24576:26624]
        vob = B[0:64, 26624:28672]
        vst = [B[:, 28672:29184], B[:, 29184:29696]]
        psum2 = [self.psA, self.psB]
        specs = self._p1_specs
        NSP = len(specs)

        cast_tok = self._p1_cast_tok
        t_evac = [0, 0]
        mm_last = {}
        tr_done = 0
        vst_tok = [0, 0]
        vst_st = [0, 0]
        t_ro_dma = 0
        t_rope_up = 0
        t_upd_prev = 0
        t_vob = 0
        t_sv = 0
        def emit_casts(j):
            for h in range(2):
                q = 2 * j + h
                t0, t1 = self._p1_ld_tok[q]
                cw = self.w((S["ld"], t0), (S["ldb"], t1))
                if (j - 2) in mm_last:
                    cw += self.w((S["pe"], mm_last[j - 2]))
                cast_tok[q] = self.dve("tensor_copy",
                                       pbf[j % 2][:, h * 2048:(h + 1) * 2048],
                                       self._p1_pf[q % 3], waits=cw)
                if self._p1_next_half < 2 * NSP:
                    self._p1_issue_half(self._p1_next_half)
                    self._p1_next_half += 1

        emit_casts(0)
        for idx, (kind, w_ap, mc) in enumerate(specs):
            par = idx % 2
            ps = 0 if kind == "v" else idx % 2
            pan_r = pbf[par].rearrange("p (kc m) -> p kc m", m=128)
            if idx + 1 < NSP:
                emit_casts(idx + 1)
            last = None
            for h in range(2):
                q = 2 * idx + h
                for kc in range(16 * h, 16 * h + 16):
                    for tb in range(NTB):
                        wl = []
                        if kc == 16 * h and tb == 0:
                            wl = self.w((S["dve"], cast_tok[q]))
                            if h == 0:
                                wl += self.w((S["dve"], t_evac[ps]))
                        last = self.mm(psum2[ps][:, tb * 512:(tb + 1) * 512],
                                       pan_r[:, kc, :],
                                       A[:, kc * T + tb * 512: kc * T + (tb + 1) * 512],
                                       start=(kc == 0), stop=(kc == KC - 1),
                                       waits=wl,
                                       inc=(kc == 16 * h + 15 and tb == NTB - 1))
            mm_last[idx] = last
            if kind == "v":
                t_ev = self.dve("tensor_copy", vo, psum2[0][:, :],
                                waits=self.w((S["pe"], last), (S["pe"], tr_done)))
                for g4 in range(4):
                    pp = g4 % 2
                    pslo = 1024 * pp
                    for j in range(4):
                        tt = 4 * g4 + j
                        trw = self.w((S["dve"], vst_tok[pp])) if j == 0 else []
                        if g4 == 0 and j == 0:
                            trw += self.w((S["dve"], t_ev))
                        tr_done = self.tr(
                            self.psB_bf[:, pslo + j * 128:pslo + (j + 1) * 128],
                            vo[:, tt * 128:(tt + 1) * 128],
                            waits=trw, inc=(j == 3))
                    t_sv = self.dve("tensor_copy", vst[pp],
                                    self.psB_bf[:, pslo:pslo + 512],
                                    waits=self.w((S["pe"], tr_done),
                                                 (S["st"], vst_st[pp])))
                    vst_tok[pp] = t_sv
                    vst_st[pp] = self.dma(
                        "gpsimd",
                        d["s_v"][4 * g4:4 * g4 + 4, :, mc * 128:(mc + 1) * 128]
                        .rearrange("tt p m -> p tt m"),
                        vst[pp].rearrange("p (tt m) -> p tt m", m=128),
                        waits=self.w((S["dve"], t_sv)), sem=S["st"])
                t_evac[0] = t_ev
                t_evac[1] = t_sv
            else:
                t_ev = self.dve("tensor_copy", vo, psum2[ps][:, :],
                                waits=self.w((S["pe"], last), (S["ld5"], t_vob)))
                k0 = vo[0:64, :]
                c, s = self.cos[0:64, :], self.sin[0:64, :]
                t_vob = self.dma("gpsimd", vob, vo[64:128, :],
                                 waits=self.w((S["dve"], t_ev),
                                              (S["dve"], t_rope_up)), sem=S["ld5"])
                self.dve("tensor_tensor", tmpA, k0, c, ALU.mult,
                         waits=self.w((S["ld5"], t_upd_prev),
                                      (S["ldc"], self._t_cossin)))
                self.dve("tensor_tensor", ro[0:64, :], vob, s, ALU.mult,
                         waits=self.w((S["ld5"], t_vob), (S["stb"], t_ro_dma)))
                self.dve("tensor_tensor", ro[0:64, :], tmpA, ro[0:64, :],
                         ALU.subtract)
                self.dve("tensor_tensor", tmpA, k0, s, ALU.mult)
                self.dve("tensor_tensor", vob, vob, c, ALU.mult)
                t_rope_up = self.dve("tensor_tensor", tmpA, tmpA, vob, ALU.add)
                t_upd_prev = self.dma("gpsimd", ro[64:128, :], tmpA,
                                      waits=self.w((S["dve"], t_rope_up)), sem=S["ld5"])
                dst = d["s_k"][mc] if kind == "k" else d["s_q"][mc]
                t_ro_dma = self.dma("scalar", dst, ro,
                                    waits=self.w((S["ld5"], t_upd_prev)), sem=S["stb"])
                t_evac[ps] = t_ev
        self.barrier()

    def _p45_stream(self):
        d, A = self.d, self.A
        S = self.sem
        qd = [A[:, 24576:32768].bitcast(I32), A[:, 32768:40960].bitcast(I32)]
        qd16 = [A[:, 24576:32768].bitcast(mybir.dt.int16)
                .rearrange("p (n two) -> p n two", two=2)[:, :, 0],
                A[:, 32768:40960].bitcast(mybir.dt.int16)
                .rearrange("p (n two) -> p n two", two=2)[:, :, 0]]
        zsr = [A[:, 40960:49152], A[:, 49152:57344]]
        wdb = [A[:, 57344:61440], A[:, 61440:65536]]
        t_tt1 = [0, 0]
        t_tt2 = [0, 0]
        wdb_st = [0, 0]
        for hc in range(HC):
            par = hc % 2
            lw = self.w((S["pool"], t_tt1[par]))
            if hc == 0:
                lw += self.w((S["ld6"], self._t_zs_all))
            t_q = self.dma("gpsimd", qd[par], d["qwd"][hc * 128:(hc + 1) * 128, :],
                           waits=lw, sem=S["ldb"])
            t_zs = self.dma("gpsimd", zsr[par],
                            d["s_zsd"][hc:hc + 1, :].rearrange("o t -> (o t)").partition_broadcast(P),
                            waits=self.w((S["pool"], t_tt2[par])), sem=S["ldb"])
            t_tt1[par] = self.pool("tensor_tensor", wdb[par], qd16[par],
                                   zsr[par][:, 0:4096], ALU.subtract,
                                   waits=self.w((S["ldb"], max(t_q, t_zs)),
                                                (S["ldb"], wdb_st[par])))
            t_tt2[par] = self.pool("tensor_tensor", wdb[par], wdb[par],
                                   zsr[par][:, 4096:8192], ALU.mult)
            wdb_st[par] = self.dma("gpsimd", d["s_wd"][hc], wdb[par],
                                   waits=self.w((S["pool"], t_tt2[par])), sem=S["ldb"])

    def p2(self):
        d, A, B = self.d, self.A, self.B
        S = self.sem
        if "mdump" in DBG_SKIP:
            self._mdump(2)
        self._p45_stream()
        for kv in range(NKV):
            self.dma("sync", B[:, kv * T:(kv + 1) * T], d["s_k"][kv])
        for tt in range(NTT):
            self.dma("sync", B[:, 16384 + tt * DKV:16384 + (tt + 1) * DKV], d["s_v"][tt])
        t_kv = S["ld"].c
        qt_r = A[:, 0:16384].rearrange("p (mc n) -> p mc n", n=512)
        et = [A[:, 16384:17408], A[:, 17408:18432]]
        ctxo = [A[:, 18432 + i * 512:18432 + (i + 1) * 512] for i in range(4)]
        lsum = A[0:1, 20480:21504].bitcast(F32)
        rsum = [A[0:1, 21504:22016], A[0:1, 22016:22528]]
        ps_sc = [self.psA[:, 0:1024], self.psB[:, 0:1024]]
        ps_ctx = [self.psA[:, 1024:1536], self.psB[:, 1024:1536]]
        ps_sum = [self.psA[0:1, 1536:2048], self.psB[0:1, 1536:2048]]

        sc_free = [0, 0]
        et_free = [0, 0]
        t_ctx_evac = [0, 0]
        t_ln = [0, 0]
        rsum_free = [0, 0]
        ctxo_st = [0] * 4
        stp = [S["st"], S["stb"]]
        t_qt_gate = (S["ld"].h, t_kv)
        pending_tail = [None]

        for qt in range(NTB):
            sq = d["s_q"].rearrange("mc p t -> p mc t")
            qw = [t_qt_gate, (S["pe"].h, S["pe"].c)]
            for g in range(4):
                t_q = self.dma("sync", qt_r[:, g * 8:(g + 1) * 8, :],
                               sq[:, g * 8:(g + 1) * 8, qt * 512:(qt + 1) * 512],
                               waits=qw)
                qw = ()
            nk = 4 * qt + 4
            npair = nk // 2
            for h in range(NH):
                kv = h // 4
                cp = h % 2
                ready = {}
                av_last = None
                sum_last = None

                def emit_sc(jp):
                    toks = []
                    for half in range(2):
                        j = 2 * jp + half
                        wl = []
                        if half == 0:
                            wl = self.w((S["act"], sc_free[jp % 2]))
                            if jp == 0:
                                wl += self.w((S["ld"], t_q))
                        toks.append(self.mm(
                            ps_sc[jp % 2][:, half * 512:(half + 1) * 512],
                            B[:, kv * T + j * 128:kv * T + (j + 1) * 128],
                            qt_r[:, h, :], start=True, stop=True,
                            waits=wl, inc=True))
                    return toks[-1]

                def emit_exp(jp, t_sc):
                    wl = self.w((S["pe"], t_sc), (S["pe"], et_free[jp % 2]))
                    tok = self.act("activation", et[jp % 2], ps_sc[jp % 2], AF.Exp,
                                   bias=self.consts[:, C_EXPBI:C_EXPBI + 1],
                                   scale=self.consts[:, C_EXPSC:C_EXPSC + 1],
                                   waits=wl)
                    sc_free[jp % 2] = tok
                    if 2 * jp >= 4 * qt:
                        mo = (2 * jp - 4 * qt) * 512
                        mt = self.dve("tensor_tensor", et[jp % 2], et[jp % 2],
                                      self.masks[:, mo:mo + 1024],
                                      ALU.mult, waits=self.w((S["act"], tok)))
                        ready[jp] = (S["dve"].h, mt)
                    else:
                        ready[jp] = (S["act"].h, tok)

                def emit_av_sum(jp):
                    nonlocal av_last, sum_last
                    for half in range(2):
                        j = 2 * jp + half
                        wl = [ready[jp]] if half == 0 else []
                        if j == 0:
                            wl += self.w((S["dve"], t_ctx_evac[cp]))
                        av_last = self.mm(
                            ps_ctx[cp],
                            B[:, 16384 + j * DKV + kv * 128:16384 + j * DKV + (kv + 1) * 128],
                            et[jp % 2][:, half * 512:(half + 1) * 512],
                            start=(j == 0), stop=(j == nk - 1),
                            waits=wl, inc=False)
                        wl2 = self.w((S["act"], t_ln[cp])) if j == 0 else ()
                        sum_last = self.mm(ps_sum[cp], self.ones[:, 0:1],
                                           et[jp % 2][:, half * 512:(half + 1) * 512],
                                           start=(j == 0), stop=(j == nk - 1),
                                           waits=wl2, inc=True)
                    et_free[jp % 2] = sum_last

                t_sc = {0: emit_sc(0)}
                emit_exp(0, t_sc[0])
                if pending_tail[0] is not None:
                    pending_tail[0]()
                    pending_tail[0] = None
                for jp in range(1, npair):
                    t_sc[jp] = emit_sc(jp)
                    emit_exp(jp, t_sc[jp])
                    emit_av_sum(jp - 1)
                emit_av_sum(npair - 1)

                t_l = self.act("activation", lsum, ps_sum[cp], AF.Ln,
                               waits=self.w((S["pe"], sum_last)))
                self.act("activation", rsum[cp], lsum, AF.Exp,
                         scale=self.consts[0:1, C_NEG1:C_NEG1 + 1],
                         waits=self.w((stp[cp], rsum_free[cp])))
                t_ln[cp] = t_l
                rsum_free[cp] = self.dma(
                    "scalar", d["s_recip"][h:h + 1, qt * 512:(qt + 1) * 512],
                    rsum[cp], sem=stp[cp])

                def mk_tail(h=h, cp=cp, qt=qt, sum_last=sum_last):
                    def f():
                        hs = h % 4
                        tok = self.dve("tensor_copy", ctxo[hs], ps_ctx[cp],
                                       waits=self.w((S["pe"], sum_last),
                                                    (stp[hs % 2], ctxo_st[hs])))
                        t_ctx_evac[cp] = tok
                        ctxo_st[hs] = self.dma(
                            "scalar", d["s_ctx"][h][:, qt * 512:(qt + 1) * 512],
                            ctxo[hs], waits=self.w((S["dve"], tok)), sem=stp[hs % 2])
                    return f
                pending_tail[0] = mk_tail()
            pending_tail[0]()
            pending_tail[0] = None
            t_qt_gate = (stp[(NH - 1) % 2].h, ctxo_st[(NH - 1) % 4])
        if "p2et" in DBG_SKIP:
            dmp = self.A[:, 20480:21504].bitcast(F32)
            seq = [("et0", et[0]), ("et1", et[1]), ("sc0", ps_sc[0]),
                   ("sc1", ps_sc[1]), ("consts", self.consts[:, 0:16]),
                   ("masks", self.masks[:, 0:512]), ("ones", self.onesrow[:, :])]
            for i, (nm, ap) in enumerate(seq):
                wl = self.w((S["ldc"], S["ldc"].c))
                if i == 0:
                    wl += self.w((stp[0], ctxo_st[0]))
                np_, nf = ap.shape[0], ap.shape[1]
                dv = dmp[0:np_, 0:nf]
                td = self.dve("tensor_copy", dv, ap, waits=wl)
                self.dma("sync", d["s_att"][8 + i][0:np_, 0:nf], dv,
                         waits=self.w((S["dve"], td)), sem=S["ldc"])
            td = self.dve("tensor_copy", dmp[0:1, :], ps_sum,
                          waits=self.w((S["ldc"], S["ldc"].c)))
            self.dma("sync", d["s_att"][5][0:1, :], dmp[0:1, :],
                     waits=self.w((S["dve"], td)), sem=S["ldc"])
            td = self.dve("tensor_copy", dmp[0:1, :], rsum,
                          waits=self.w((S["ldc"], S["ldc"].c)))
            self.dma("sync", d["s_att"][6][0:1, :], dmp[0:1, :],
                     waits=self.w((S["dve"], td)), sem=S["ldc"])
        self.barrier()

    def p3(self):
        d, A, B = self.d, self.A, self.B
        S = self.sem
        pf = [B[:, 0:4096].bitcast(F32), B[:, 4096:8192].bitcast(F32)]
        pbf = [B[:, 8192:12288], B[:, 12288:16384]]
        xoc = B[:, 16384:18432]
        xres_pair = [B[:, 18432:20480], B[:, 20480:22528]]
        sqb = [B[:, 22528:23040], B[:, 23040:23552]]
        statsf = B[:, 23552:27648].bitcast(F32)
        recip_rep = B[:, 27648:29696]
        rowsf = B[0:1, 0:12288].bitcast(F32)
        arow = B[0:1, 12288:14336]
        brow = B[0:1, 14336:16384]
        wor = d["wo"].rearrange("(kc p) m -> p kc m", p=P)

        ld_tok = {}
        cast_tok = {}

        def issue_half(q):
            idx, h = q // 2, q % 2
            dst = pf[q % 2].rearrange("p (kc m) -> p kc m", m=128)
            wl = self.w((S["dve"], cast_tok[q - 2])) if (q - 2) in cast_tok else ()
            t0 = self.dma("sync", dst[:, 0:8, :],
                          wor[:, 16 * h:16 * h + 8, idx * 128:(idx + 1) * 128],
                          waits=wl, sem=S["ld"])
            t1 = self.dma("scalar", dst[:, 8:16, :],
                          wor[:, 16 * h + 8:16 * h + 16, idx * 128:(idx + 1) * 128],
                          waits=wl, sem=S["ldb"])
            ld_tok[q] = (t0, t1)

        issue_half(0)
        issue_half(1)
        next_half = [2]
        t_rn = {}
        t_prev = 0
        for kc in range(KC):
            self.dma("gpsimd", A[:, kc * T:(kc + 1) * T], d["s_ctx"][kc], sem=S["ldc"])
            t_r = self.dma("gpsimd", recip_rep,
                           d["s_recip"][kc:kc + 1, :].rearrange("o t -> (o t)")
                           .partition_broadcast(P),
                           waits=(self.w((S["pool"], t_prev)) if kc else ()),
                           sem=S["ldc"])
            t_prev = self.pool("tensor_tensor", A[:, kc * T:(kc + 1) * T],
                               A[:, kc * T:(kc + 1) * T], recip_rep, ALU.mult,
                               waits=self.w((S["ldc"], t_r)))
            t_rn[kc] = t_prev

        st_ = {"evac": [0] * NTB, "summ": [0] * NTB, "sq_last": 0, "xres_st": [0, 0]}
        pending = [None]
        mm_last = {}
        t_xoc = 0

        def emit_casts(j):
            for h in range(2):
                q = 2 * j + h
                t0, t1 = ld_tok[q]
                cw = self.w((S["ld"], t0), (S["ldb"], t1))
                if (j - 2) in mm_last:
                    cw += self.w((S["pe"], mm_last[j - 2]))
                cast_tok[q] = self.dve("tensor_copy",
                                       pbf[j % 2][:, h * 2048:(h + 1) * 2048],
                                       pf[q % 2], waits=cw)
                if next_half[0] < 2 * KC:
                    issue_half(next_half[0])
                    next_half[0] += 1

        emit_casts(0)
        for oc in range(KC):
            par = oc % 2
            pan_r = pbf[par].rearrange("p (kc m) -> p kc m", m=128)
            if oc + 1 < KC:
                emit_casts(oc + 1)
            t_xoc = self.dma("gpsimd", xoc, d["s_xT"][oc],
                             waits=(self.w((S["dve"], st_["evac"][NTB - 1]))
                                    if oc > 0 else ()), sem=S["ldc"])
            mm_toks = [None] * NTB
            te_list = [None] * NTB
            for hp in range(2):
                last = None
                for kc in range(KC):
                    for tb in (2 * hp, 2 * hp + 1):
                        wl = []
                        if kc == 0 and tb == 2 * hp:
                            wl = self.w((S["dve"], cast_tok[2 * oc]),
                                        (S["dve"], st_["evac"][2 * hp + 1]))
                        elif kc == 16 and tb == 2 * hp:
                            wl = self.w((S["dve"], cast_tok[2 * oc + 1]))
                        if oc == 0:
                            wl += self.w((S["pool"], t_rn[kc]))
                        last = self.mm(self.psA[:, tb * 512:(tb + 1) * 512],
                                       pan_r[:, kc, :],
                                       A[:, kc * T + tb * 512:kc * T + (tb + 1) * 512],
                                       start=(kc == 0), stop=(kc == KC - 1),
                                       waits=wl, inc=(kc == KC - 1))
                        mm_toks[tb] = last
                for tb in (2 * hp, 2 * hp + 1):
                    te = self.dve("tensor_tensor",
                                  xres_pair[par][:, tb * 512:(tb + 1) * 512],
                                  self.psA[:, tb * 512:(tb + 1) * 512],
                                  xoc[:, tb * 512:(tb + 1) * 512], ALU.add,
                                  waits=self.w((S["pe"], mm_toks[tb]),
                                               (S["ldc"], t_xoc),
                                               (S["pe"], st_["summ"][tb]),
                                               (S["st"], st_["xres_st"][par])))
                    st_["evac"][tb] = te
                    te_list[tb] = te
                if hp == 0 and pending[0] is not None:
                    pending[0]()
                    pending[0] = None
            mm_last[oc] = mm_toks[NTB - 1]

            def mk(oc=oc, te_list=te_list, xres=xres_pair[par]):
                def f():
                    for tb in range(NTB):
                        tq = self.dve("tensor_tensor", sqb[tb % 2],
                                      xres[:, tb * 512:(tb + 1) * 512],
                                      xres[:, tb * 512:(tb + 1) * 512], ALU.mult,
                                      waits=self.w((S["dve"], te_list[tb]),
                                                   (S["pe"], st_["sq_last"])))
                        st_["summ"][tb] = self.mm(
                            self.psB[0:1, tb * 512:(tb + 1) * 512],
                            self.ones[:, 0:1], xres[:, tb * 512:(tb + 1) * 512],
                            start=(oc == 0), stop=(oc == KC - 1),
                            waits=self.w((S["dve"], te_list[tb])), inc=True)
                        st_["sq_last"] = self.mm(
                            self.psB[32:33, tb * 512:(tb + 1) * 512],
                            self.ones[:, 0:1], sqb[tb % 2],
                            start=(oc == 0), stop=(oc == KC - 1),
                            waits=self.w((S["dve"], tq)), inc=True)
                return f
            pending[0] = mk()
            if oc == KC - 1:
                pending[0]()
                pending[0] = None
            st_["xres_st"][par] = self.dma(
                "gpsimd", d["s_xres"][oc], xres_pair[par],
                waits=self.w((S["dve"], st_["evac"][NTB - 1])), sem=S["st"])
        t_st = self.dve("tensor_copy", statsf[0:64, :], self.psB[0:64, :],
                        waits=self.w((S["pe"], st_["sq_last"])))
        sums, sumsq = statsf[0:1, :], statsf[32:33, :]
        m, ex2 = rowsf[:, 0:2048], rowsf[:, 2048:4096]
        tmp = rowsf[:, 4096:6144]
        c = self.consts
        self.dve("tensor_scalar", m, sums, c[0:1, C_INVD:C_INVD + 1], None, ALU.mult)
        self.dve("tensor_scalar", ex2, sumsq, c[0:1, C_INVD:C_INVD + 1], None, ALU.mult)
        self.dve("tensor_tensor", tmp, m, m, ALU.mult)
        t_var = self.dve("tensor_tensor", ex2, ex2, tmp, ALU.subtract)
        t_l = self.act("activation", tmp, ex2, AF.Ln,
                       bias=c[0:1, C_EPS:C_EPS + 1], waits=self.w((S["dve"], t_var)))
        t_r = self.act("activation", ex2, tmp, AF.Exp,
                       scale=c[0:1, C_NHALF:C_NHALF + 1])
        t_a = self.dve("tensor_copy", arow, ex2, waits=self.w((S["act"], t_r)))
        self.dve("tensor_tensor", tmp, m, ex2, ALU.mult)
        t_b = self.dve("tensor_scalar", brow, tmp, c[0:1, C_NEG1:C_NEG1 + 1], None,
                       ALU.mult)
        self.dma("gpsimd", d["s_ab"][0:1, :], arow, waits=self.w((S["dve"], t_a)),
                 sem=S["st"])
        self.dma("gpsimd", d["s_ab"][1:2, :], brow, waits=self.w((S["dve"], t_b)),
                 sem=S["st"])
        self.barrier()

    def p4pre(self):
        d, A, B = self.d, self.A, self.B
        S = self.sem
        a_rep, b_rep = B[:, 0:2048], B[:, 2048:4096]
        t_a = self.dma("sync", a_rep, d["s_ab"][0:1, :].rearrange("o t -> (o t)").partition_broadcast(P))
        t_b = self.dma("sync", b_rep, d["s_ab"][1:2, :].rearrange("o t -> (o t)").partition_broadcast(P))
        t_lds = [self.dma("sync", A[:, kc * T:(kc + 1) * T], d["s_xres"][kc])
                 for kc in range(KC)]
        for kc in range(KC):
            sl = A[:, kc * T:(kc + 1) * T]
            self.dve("tensor_tensor", sl, sl, a_rep, ALU.mult,
                     waits=self.w((S["ld"], t_lds[-1])))
            tx = self.dve("tensor_tensor", sl, sl, b_rep, ALU.add)
            if "x1" in self.debug:
                self.dma("sync", d["s_x1"][kc], sl, waits=self.w((S["dve"], tx)),
                         sem=S["st"])
        self.barrier()

    def p4(self):
        d, A, B = self.d, self.A, self.B
        S = self.sem
        self.load_act_set(18)
        qbuf = B[:, 0:8192].bitcast(I32)
        qbuf_r = qbuf.rearrange("p (kc m) -> p kc m", m=128)
        qbuf_i16 = B[:, 0:8192].bitcast(mybir.dt.int16) \
            .rearrange("p (n two) -> p n two", two=2)[:, :, 0]
        zpart = B[:, 8192:12288]
        spart = B[:, 12288:16384]
        zs = B[:, 8192:16384]
        wdeq = {"g": B[:, 16384:20480], "u": B[:, 20480:24576]}
        ho = [B[:, 24576:26624], B[:, 26624:28672]]
        sg = B[:, 28672:30720]
        qws = {"g": d["qwg"].rearrange("(kc p) n -> p kc n", p=P),
               "u": d["qwu"].rearrange("(kc p) n -> p kc n", p=P)}
        zss = {"g": d["s_zsg"], "u": d["s_zsu"]}
        psm = {"g": self.psA, "u": self.psB}

        t_tt1 = 0
        t_tt2 = 0
        wdeq_free = {"g": 0, "u": 0}
        silu_done = [0] * NTB
        tth_done = [0] * NTB
        ho_st = [0, 0]
        stp = [S["st"], S["stb"]]
        for hc in range(HC):
            mm_tok = {}
            for m in ("g", "u"):
                ldt = []
                for hh in range(2):
                    ldt.append(self.dma(
                        "sync", qbuf_r[:, 16 * hh:16 * hh + 16, :],
                        qws[m][:, 16 * hh:16 * hh + 16, hc * 128:(hc + 1) * 128],
                        waits=(self.w((S["pool"], t_tt1)) if hh == 0 else ())))
                zw = self.w((S["pool"], t_tt2))
                if hc == 0:
                    zw += self.w((S["ld6"], self._t_zs_gu))
                t_zs = self.dma("sync", zs,
                                zss[m][hc:hc + 1, :].rearrange("o t -> (o t)").partition_broadcast(P),
                                waits=zw)
                t_tt1 = self.pool("tensor_tensor", wdeq[m], qbuf_i16, zpart, ALU.subtract,
                                  waits=self.w((S["ld"], max(ldt[-1], t_zs)),
                                               (S["pe"], wdeq_free[m])))
                t_tt2 = self.pool("tensor_tensor", wdeq[m], wdeq[m], spart, ALU.mult)
                toks = []
                for tb in range(NTB):
                    last = None
                    for kc in range(KC):
                        wl = []
                        if kc == 0:
                            wl = self.w((S["pool"], t_tt2))
                            if m == "g":
                                wl += self.w((S["act"], silu_done[tb]))
                            else:
                                wl += self.w((S["dve"], tth_done[tb]))
                        last = self.mm(psm[m][:, tb * 512:(tb + 1) * 512],
                                       wdeq[m][:, kc * 128:(kc + 1) * 128],
                                       A[:, kc * T + tb * 512:kc * T + (tb + 1) * 512],
                                       start=(kc == 0), stop=(kc == KC - 1),
                                       waits=wl, inc=(kc == KC - 1))
                    toks.append(last)
                mm_tok[m] = toks
                wdeq_free[m] = toks[-1]
            par = hc % 2
            tth = []
            for tb in range(NTB):
                ts = self.act("activation", sg[:, tb * 512:(tb + 1) * 512],
                              self.psA[:, tb * 512:(tb + 1) * 512], AF.Silu,
                              waits=self.w((S["pe"], mm_tok["g"][tb]),
                                           (S["dve"], tth_done[tb])))
                silu_done[tb] = ts
                th = self.dve("tensor_tensor", ho[par][:, tb * 512:(tb + 1) * 512],
                              self.psB[:, tb * 512:(tb + 1) * 512],
                              sg[:, tb * 512:(tb + 1) * 512], ALU.mult,
                              waits=self.w((S["pe"], mm_tok["u"][tb]),
                                           (S["act"], ts), (stp[par], ho_st[par])))
                tth_done[tb] = th
                tth.append(th)
            ho_st[par] = self.dma("scalar", d["s_h"][hc], ho[par],
                                  waits=self.w((S["dve"], tth[-1])), sem=stp[par])
        self.barrier()

    def p45(self):
        d, A, B = self.d, self.A, self.B
        S = self.sem
        qd = B[:, 0:8192].bitcast(I32)
        zpart = B[:, 8192:12288]
        spart = B[:, 12288:16384]
        zs = B[:, 8192:16384]
        wdb = [B[:, 16384:20480], B[:, 20480:24576]]
        t_tt1 = 0
        t_tt2 = 0
        wdb_st = [0, 0]
        stp = [S["st"], S["stb"]]
        for hc in range(HC):
            par = hc % 2
            t_q = self.dma("sync", qd, d["qwd"][hc * 128:(hc + 1) * 128, :],
                           waits=self.w((S["dve"], t_tt1)))
            t_zs = self.dma("sync", zs, d["s_zsd"][hc:hc + 1, :].rearrange("o t -> (o t)").partition_broadcast(P),
                            waits=self.w((S["dve"], t_tt2)))
            t_tt1 = self.dve("tensor_tensor", wdb[par], qd, zpart, ALU.subtract,
                             waits=self.w((S["ld"], max(t_q, t_zs)),
                                          (stp[par], wdb_st[par])))
            t_tt2 = self.dve("tensor_tensor", wdb[par], wdb[par], spart, ALU.mult)
            wdb_st[par] = self.dma("sync", d["s_wd"][hc], wdb[par],
                                   waits=self.w((S["dve"], t_tt2)), sem=stp[par])
        self.barrier()

    def p5(self):
        d, A, B = self.d, self.A, self.B
        S = self.sem
        NR = 8
        ht = [B[:, i * 1024:(i + 1) * 1024] for i in range(NR)]
        wt = [B[:, 8192 + i * 512:8192 + (i + 1) * 512] for i in range(NR)]
        psY = [self.psA[:, 0:1024], self.psA[:, 1024:2048],
               self.psB[:, 0:1024], self.psB[:, 1024:2048]]
        buf_free = [0] * NR
        t_ev = [0] * 4
        ldp = [S["ld"], S["ldb"], S["ldc"], S["ld3"], S["ld4"], S["ld5"], S["ld6"], S["ld7"]]
        for dq in range(8):
            for th in range(2):
                hc_tok = {}
                for hc in range(HC):
                    par = hc % NR
                    t_h = self.dma("sync", ht[par],
                                   d["s_h"][hc][:, th * 1024:(th + 1) * 1024],
                                   waits=self.w((S["pe"], buf_free[par])),
                                   sem=ldp[par])
                    t_w = self.dma("scalar", wt[par],
                                   d["s_wd"][hc][:, dq * 512:(dq + 1) * 512],
                                   waits=self.w((S["pe"], buf_free[par])),
                                   sem=ldp[par])
                    last = None
                    for dc in range(4):
                        for tb2 in range(2):
                            wl = []
                            if dc == 0 and tb2 == 0:
                                wl = self.w((ldp[par], max(t_h, t_w)))
                                if hc == 0:
                                    wl += self.w((S["dve"], max(t_ev)))
                            last = self.mm(
                                psY[dc][:, tb2 * 512:(tb2 + 1) * 512],
                                wt[par][:, dc * 128:(dc + 1) * 128],
                                ht[par][:, tb2 * 512:(tb2 + 1) * 512],
                                start=(hc == 0), stop=(hc == HC - 1),
                                waits=wl, inc=(dc == 3 and tb2 == 1))
                    buf_free[par] = last
                    hc_tok[hc] = last
                for dc in range(4):
                    dcg = dq * 4 + dc
                    sl = A[:, dcg * T + th * 1024:dcg * T + (th + 1) * 1024]
                    t_ev[dc] = self.dve("tensor_tensor", sl, psY[dc], sl, ALU.add,
                                        waits=self.w((S["pe"], hc_tok[HC - 1])))
        if "x2" in self.debug:
            tx = S["dve"].c
            for kc in range(KC):
                self.dma("sync", d["s_x2"][kc], A[:, kc * T:(kc + 1) * T],
                         waits=self.w((S["dve"], tx)), sem=S["st"])
        self.barrier()

    def p6(self):
        d, A, B = self.d, self.A, self.B
        S = self.sem
        self.load_act_set(6)
        sq = [B[:, 3072:3584], B[:, 3584:4096]]
        statsf = B[:, 12800:16896].bitcast(F32)
        rowsf = B[0:1, 16896:29184].bitcast(F32)
        c = self.consts
        t_sqmm = [0, 0]
        t_last = None
        for tb in range(NTB):
            for kc in range(KC):
                par = kc % 2
                sl = A[:, kc * T + tb * 512:kc * T + (tb + 1) * 512]
                tq = self.dve("tensor_tensor", sq[par], sl, sl, ALU.mult,
                              waits=self.w((S["pe"], t_sqmm[par])))
                self.mm(self.psA[0:1, tb * 512:(tb + 1) * 512],
                        self.ones[:, 0:1], sl,
                        start=(kc == 0), stop=(kc == KC - 1), waits=(), inc=False)
                t_sqmm[par] = self.mm(self.psA[32:33, tb * 512:(tb + 1) * 512],
                                      self.ones[:, 0:1], sq[par],
                                      start=(kc == 0), stop=(kc == KC - 1),
                                      waits=self.w((S["dve"], tq)), inc=True)
                t_last = t_sqmm[par]
        t_stats = self.dve("tensor_copy", statsf[0:64, :], self.psA[0:64, :],
                           waits=self.w((S["pe"], t_last)))
        sums, sumsq = statsf[0:1, :], statsf[32:33, :]
        m, ex2 = rowsf[:, 0:2048], rowsf[:, 2048:4096]
        tmp = rowsf[:, 4096:6144]
        self.dve("tensor_scalar", m, sums, c[0:1, C_INVD:C_INVD + 1], None, ALU.mult)
        self.dve("tensor_scalar", ex2, sumsq, c[0:1, C_INVD:C_INVD + 1], None, ALU.mult)
        self.dve("tensor_tensor", tmp, m, m, ALU.mult)
        t_var = self.dve("tensor_tensor", ex2, ex2, tmp, ALU.subtract)
        self.act("activation", tmp, ex2, AF.Ln,
                 bias=c[0:1, C_EPS:C_EPS + 1], waits=self.w((S["dve"], t_var)))
        t_r = self.act("activation", ex2, tmp, AF.Exp,
                       scale=c[0:1, C_NHALF:C_NHALF + 1])
        t_nm = self.dve("tensor_tensor", tmp, m, ex2, ALU.mult,
                        waits=self.w((S["act"], t_r)))
        t_nm = self.dve("tensor_scalar", tmp, tmp, c[0:1, C_NEG1:C_NEG1 + 1], None,
                        ALU.mult)
        t_s1 = self.dma("sync", d["s_ln2"][0:1, :], ex2,
                        waits=self.w((S["act"], t_r)), sem=S["st"])
        t_s2 = self.dma("sync", d["s_ln2"][1:2, :], tmp,
                        waits=self.w((S["dve"], t_nm)), sem=S["st"])
        rcol = B[:, 25088:25120].bitcast(F32)
        nmrcol = B[:, 25120:25152].bitcast(F32)
        t_rc = self.dma("sync", rcol,
                        d["s_ln2"][0:1, :].rearrange("o (tt p) -> (o p) tt", p=P),
                        waits=self.w((S["st"], t_s1)),
                        sem=S["ldc"], allow_slow_non_contiguous=True)
        t_nc = self.dma("sync", nmrcol,
                        d["s_ln2"][1:2, :].rearrange("o (tt p) -> (o p) tt", p=P),
                        waits=self.w((S["st"], t_s2)),
                        sem=S["ldc"], allow_slow_non_contiguous=True)
        outb = [B[:, 4096:12288].bitcast(F32), B[:, 16896:25088].bitcast(F32)]
        psbf = [self.psA_bf, self.psB_bf]
        ps_free = [(S["dve"].h, t_stats), (S["dve"].h, t_stats)]
        out_st = [0, 0]
        stp = [S["st"], S["stb"]]
        first = self.w((S["ldc"], max(t_rc, t_nc)))
        for tt in range(NTT):
            par = tt % 2
            t_aff = []
            for kcg in range(8):
                pp = kcg % 2
                last_tr = None
                for j in range(4):
                    kc = 4 * kcg + j
                    last_tr = self.tr(psbf[pp][:, j * 128:(j + 1) * 128],
                                      A[:, kc * T + tt * 128:kc * T + (tt + 1) * 128],
                                      waits=([ps_free[pp]] if j == 0 else ()),
                                      inc=(j == 3))
                wl = first + self.w((S["pe"], last_tr), (stp[par], out_st[par]))
                first = []
                ta = self.dve("tensor_scalar",
                              outb[par][:, kcg * 512:(kcg + 1) * 512],
                              psbf[pp][:, 0:512],
                              rcol[:, tt:tt + 1], nmrcol[:, tt:tt + 1],
                              ALU.mult, ALU.add, waits=wl)
                ps_free[pp] = (S["dve"].h, ta)
                t_aff.append(ta)
            out_st[par] = self.dma("sync", d["out"][tt * 128:(tt + 1) * 128, :],
                                   outb[par], waits=self.w((S["dve"], t_aff[-1])),
                                   sem=stp[par])
        self._wait("sync", S["st"], S["st"].c)
        self._wait("sync", S["stb"], S["stb"].c)
        self.barrier()

    def finalize(self):
        nc = self.nc
        with nc.Block() as block:
            for engn in ENGINES:
                def runner(e, engn=engn):
                    for f in self.ops[engn]:
                        f(e)
                getattr(block, engn)(runner)
        self.es.close()
        return nc


def build(debug=(), nphases=9):
    b = Builder(debug)
    phases = [b.p0, b.p1, b.p2, b.p3, b.p4pre, b.p4, b.p5, b.p6]
    for p in phases[:nphases]:
        p()
    return b.finalize()


def _rope_tables():
    inv = 1.0 / (500000.0 ** (np.arange(0, 128, 2, dtype=np.float64) / 128.0))
    t = np.arange(T, dtype=np.float64)
    freqs = np.outer(t, inv)
    cos = np.cos(freqs).T.astype(np.float32)
    sin = np.sin(freqs).T.astype(np.float32)
    return np.ascontiguousarray(cos), np.ascontiguousarray(sin)


_NC_CACHE = {}


def kernel(L0_attn_x, L0_attn_Wq, L0_attn_Wk, L0_attn_Wv, L0_attn_Wo,
           L0_ffn_qw_gate, L0_ffn_z_gate, L0_ffn_s_gate,
           L0_ffn_qw_up, L0_ffn_z_up, L0_ffn_s_up,
           L0_ffn_qw_down, L0_ffn_z_down, L0_ffn_s_down):
    if "nc" not in _NC_CACHE:
        _NC_CACHE["nc"] = build()
    nc = _NC_CACHE["nc"]
    cos, sin = _rope_tables()
    common = {
        "wq": np.ascontiguousarray(L0_attn_Wq, dtype=np.float32),
        "wk": np.ascontiguousarray(L0_attn_Wk, dtype=np.float32),
        "wv": np.ascontiguousarray(L0_attn_Wv, dtype=np.float32),
        "wo": np.ascontiguousarray(L0_attn_Wo, dtype=np.float32),
        "qwg": np.ascontiguousarray(L0_ffn_qw_gate, dtype=np.int32),
        "zg": np.ascontiguousarray(L0_ffn_z_gate, dtype=np.int32),
        "sg": np.ascontiguousarray(L0_ffn_s_gate, dtype=np.float32),
        "qwu": np.ascontiguousarray(L0_ffn_qw_up, dtype=np.int32),
        "zu": np.ascontiguousarray(L0_ffn_z_up, dtype=np.int32),
        "su": np.ascontiguousarray(L0_ffn_s_up, dtype=np.float32),
        "qwd": np.ascontiguousarray(L0_ffn_qw_down, dtype=np.int32),
        "zd": np.ascontiguousarray(L0_ffn_z_down, dtype=np.int32),
        "sd": np.ascontiguousarray(L0_ffn_s_down, dtype=np.float32),
        "cosq": cos, "sinq": sin,
    }
    x = np.asarray(L0_attn_x, dtype=np.float32)
    in_maps = [dict(common, x=np.ascontiguousarray(x[i])) for i in range(8)]
    res = run_bass_kernel_spmd(nc, in_maps, core_ids=list(range(8)))
    return np.stack([res.results[i]["out"] for i in range(8)], axis=0)
